# version 43
# baseline (speedup 1.0000x reference)
import numpy as np

import concourse.bass as bass
import concourse.mybir as mybir
from concourse.bass_utils import run_bass_kernel_spmd

F32 = mybir.dt.float32
BF16 = mybir.dt.bfloat16
I32 = mybir.dt.int32
AF = mybir.ActivationFunctionType
ALU = mybir.AluOpType

T = 2048
D = 1024
H = 8
DH = 256
LR = 64
FH = 3072
NCH = 8
NTB = 4
NKC = 16
NFC = 24
NHALF = 12
INV127 = 1.0 / 127.0
SC_ATTN = 1.0 / 16.0
EPS = 1e-5

XH = 0
RA = 16384
QH = 32768
KH = 36864
VH = 40960
ZT = 45072
ZF = 49168
WQ0, WQ1 = 53264, 55312
WO0, WO1 = 57360, 59408
WKU = 43008
WVU = 44048
WKV = 50192
KVA = 50704
IDC = 52752
ONE = 53248
ARENA = 61456
QP = QH
KVT = KH
WAB = KH + 1040
WAT = (KH + 2064, KH + 2576)
MSB = (VH, VH + 1024)
UTM = 45072
UFM = (46096, 48144)
HHB = RA
WDB = VH
WG = (WQ0, WQ0 + 1024)
WU = (WQ1, WQ1 + 1024)
SCR_S = WO0
SCR_U = WO0 + 1024


class _Sem:
    __slots__ = ("h", "step", "n")


class _Builder:
    def __init__(self, nc):
        self.nc = nc
        self.q = {"tensor": [], "scalar": [], "vector": [], "gpsimd": [], "sync": []}
        self.waited = {k: {} for k in self.q}
        self.selfsem = {}

    def sem(self, name, step=1):
        s = _Sem()
        s.h = self.nc.alloc_semaphore(name)
        s.step = step
        s.n = 0
        return s

    def emit(self, eng, fn, waits=(), inc=None, selfsync=True):
        ss = self.selfsem.get(eng)
        if ss is not None and selfsync:
            waits = list(waits) + ([(ss, ss.n)] if ss.n > 0 else [])
            if inc is None:
                inc = ss
            else:
                assert inc is ss, f"{eng} compute ops must inc the engine sem"
        wl = []
        for (s, v) in waits:
            if s is None or v is None or v <= 0:
                continue
            assert v <= s.n, f"wait beyond emitted count on {eng}: {v} > {s.n}"
            w = self.waited[eng]
            k = id(s)
            if v > w.get(k, 0):
                w[k] = v
                wl.append((s.h, v))
        after = None
        if inc is not None:
            inc.n += inc.step
            after = (inc.h, inc.step)

        def run(e, wl=wl, fn=fn, after=after):
            for (hh, v) in wl:
                e.wait_ge(hh, v)
            inst = fn(e)
            if after is not None:
                inst.then_inc(after[0], after[1])

        self.q[eng].append(run)
        return inc.n if inc is not None else None


def build_program(tap=None):
    nc = bass.Bass("TRN2", target_bir_lowering=False)

    x_d = nc.dram_tensor("x", [NTB, 128, NCH, 512], F32, kind="ExternalInput").ap()
    wq_d = nc.dram_tensor("wq", [H, 128, 2048], F32, kind="ExternalInput").ap()
    wkv_d = nc.dram_tensor("wkv", [128, 512], F32, kind="ExternalInput").ap()
    wku_d = nc.dram_tensor("wku", [128, 1024], F32, kind="ExternalInput").ap()
    wvu_d = nc.dram_tensor("wvu", [128, 1024], F32, kind="ExternalInput").ap()
    wo_d = nc.dram_tensor("wo", [H, 128, 2048], F32, kind="ExternalInput").ap()
    wg_d = nc.dram_tensor("wg", [NFC, 128, 1024], I32, kind="ExternalInput").ap()
    wu_d = nc.dram_tensor("wu", [NFC, 128, 1024], I32, kind="ExternalInput").ap()
    wd_d = nc.dram_tensor("wd", [NFC, 128, D], I32, kind="ExternalInput").ap()
    out_d = nc.dram_tensor("out", [D, T], F32, kind="ExternalOutput").ap()

    x_t = nc.alloc_sbuf_tensor("x_fm", [128, NCH * T], F32)
    ar_t = nc.alloc_sbuf_tensor("arena", [128, ARENA], BF16)
    t1_t = nc.alloc_sbuf_tensor("t1", [128, 16], F32)
    rows_t = nc.alloc_sbuf_tensor("rows", [1, 3072], F32)
    rc_t = nc.alloc_sbuf_tensor("rc", [128, 1], F32)
    mbc_t = nc.alloc_sbuf_tensor("mbc", [128, 1024], BF16)
    rbc_t = nc.alloc_sbuf_tensor("rbc", [128, 1024], BF16)
    onesb_row_t = nc.alloc_sbuf_tensor("onesb_row", [1, 128], BF16)
    x = x_t.ap()
    ar = ar_t.ap()
    t1 = t1_t.ap()
    rows = rows_t.ap()
    rc = rc_t.ap()
    mbc = mbc_t.ap()
    rbc = rbc_t.ap()
    onesb_row = onesb_row_t.ap()

    PA = nc.alloc_psum_tensor("PA", [128, 1024], F32).ap()
    PB = nc.alloc_psum_tensor("PB", [128, 1024], F32).ap()
    PC = nc.alloc_psum_tensor("PC", [128, 1024], F32).ap()
    PD = nc.alloc_psum_tensor("PD", [128, 1024], F32).ap()

    def bank(P, i):
        return P[:, i * 512:(i + 1) * 512]

    b = _Builder(nc)
    sX = [b.sem(f"sX{i}", 16) for i in range(4)]
    sWS = b.sem("sWS", 16)
    sXB = b.sem("sXB", 16)
    sXB2 = b.sem("sXB2", 16)
    sWQ = [b.sem("sWQ0", 16), b.sem("sWQ1", 16)]
    sWO = [b.sem("sWO0", 16), b.sem("sWO1", 16)]
    sWG = [b.sem("sWG0", 16), b.sem("sWG1", 16)]
    sWU = [b.sem("sWU0", 16), b.sem("sWU1", 16)]
    sWD = b.sem("sWD", 16)
    sST = b.sem("sST", 16)
    sPE = b.sem("sPE", 1)
    sACT = b.sem("sACT", 1)
    sDVE = b.sem("sDVE", 1)
    sPOOL = b.sem("sPOOL", 1)
    b.selfsem = {"vector": sDVE, "scalar": sACT, "gpsimd": sPOOL}

    bank_free = {}

    def pe_group(out_ap, keys, pieces, waits=()):
        w = list(waits)
        for k in keys:
            bf = bank_free.get(k)
            if bf:
                w.append(bf)
        n = len(pieces)
        for i, (l, r) in enumerate(pieces):
            b.emit(
                "tensor",
                (lambda e, o=out_ap, l=l, r=r, st=(i == 0), sp=(i == n - 1):
                 e.matmul(o, l, r, start=st, stop=sp)),
                waits=w if i == 0 else (),
                inc=sPE if i == n - 1 else None,
            )
        return sPE.n

    def consume(eng, fn, keys, waits, inc):
        v = b.emit(eng, fn, waits=waits, inc=inc)
        for k in keys:
            bank_free[k] = (inc, v)
        return v

    def maybe_tap(name, src_ap, waits):
        if tap == name:
            td = nc.dram_tensor("tap", list(src_ap.shape), src_ap.dtype,
                                kind="ExternalOutput").ap()
            b.emit("sync", lambda e, s=src_ap, t=td: e.dma_start(out=t, in_=s),
                   waits=waits, inc=sST)

    wz = b.emit("gpsimd", lambda e: e.memset(ar[:, QH:QH + 512], 0.0), inc=sPOOL)
    ident = ar[:, IDC:IDC + 128]
    ones_bf = ar[:, ONE:ONE + 1]
    b.emit("gpsimd", lambda e: e.memset(ident, 0.0), inc=sPOOL)
    b.emit("gpsimd", lambda e: e.affine_select(
        out=ident, in_=ident,
        compare_op=ALU.not_equal, fill=1.0, base=0,
        pattern=[[-1, 128]], channel_multiplier=1), inc=sPOOL)
    b.emit("gpsimd", lambda e: e.memset(ones_bf, 1.0), inc=sPOOL)
    b.emit("gpsimd", lambda e: e.memset(onesb_row, 1.0), inc=sPOOL)
    kvt1 = ar[:, KVT:KVT + NKC * 65].rearrange("p (t c) -> p t c", c=65)[:, :, 64:65]
    b.emit("gpsimd", lambda e: e.memset(kvt1, 1.0), inc=sPOOL)
    b.emit("gpsimd", lambda e: e.memset(ar[64:128, KVA:KVA + 2048], 0.0), inc=sPOOL)
    pool_init = sPOOL.n
    for tb in range(NTB):
        b.emit("sync", lambda e, tb=tb: e.dma_start(
            out=x.rearrange("p (c n) -> p c n", c=NCH)
                [:, :, tb * 512:(tb + 1) * 512],
            in_=x_d[tb]),
            inc=sX[tb])
    for i in range(63):
        b.emit("tensor", lambda e: e.matmul(
            PD[:, 0:512], ar[:, QH:QH + 128], ar[:, QH:QH + 512],
            start=True, stop=True),
            waits=[(sPOOL, wz)] if i == 0 else (), inc=None)
    b.emit("tensor", lambda e: e.matmul(
        PD[:, 0:512], ar[:, QH:QH + 128], ar[:, QH:QH + 512],
        start=True, stop=True), inc=sPE)
    warm_done = sPE.n
    bank_free[("PD", 0)] = (sPE, warm_done)
    b.emit("gpsimd", lambda e: e.dma_start(
        out=ar[:, WKV:WKV + 512], in_=wkv_d), inc=sWS, selfsync=False)
    b.emit("gpsimd", lambda e: e.dma_start(
        out=ar[:, WKU:WKU + 1024], in_=wku_d), inc=sWS, selfsync=False)
    b.emit("gpsimd", lambda e: e.dma_start(
        out=ar[:, WVU:WVU + 1024], in_=wvu_d), inc=sWS, selfsync=False)
    ws_all = sWS.n

    wq_base = [WQ0, WQ1]
    wo_base = [WO0, WO1]

    def load_wq(h, extra_waits=()):
        b.emit("gpsimd", lambda e, h=h: e.dma_start(
            out=ar[:, wq_base[h % 2]:wq_base[h % 2] + 2048], in_=wq_d[h]),
            waits=extra_waits, inc=sWQ[h % 2], selfsync=False)

    def load_wo(h, extra_waits=()):
        b.emit("gpsimd", lambda e, h=h: e.dma_start(
            out=ar[:, wo_base[h % 2]:wo_base[h % 2] + 2048], in_=wo_d[h]),
            waits=extra_waits, inc=sWO[h % 2], selfsync=False)

    load_wq(0)
    load_wo(0)
    load_wq(1)
    load_wo(1)
    b.emit("gpsimd", lambda e: e.memset(ar[64:128, QP:QP + 4096], 0.0),
           waits=[(sPE, warm_done)], inc=sPOOL, selfsync=False)
    pool_qp = sPOOL.n

    xh_attn = []
    xh_ffn = []
    kv_ready_tb = [None] * NTB
    kvt_ready = [None]
    qpev = [[None] * NTB for _ in range(H)]
    prepA_ev = [None] * H
    wat_done = [None, None]
    msb_done = [None] * 4
    ut_done = [[None, None] for _ in range(2)]
    q_mm_done = [0] * H
    wo_mm_done = [0] * H
    wo_add_last = [0] * NCH
    pb_rot = [0]
    s_rot = [0]
    wo_pair = [0]
    exp_done = {}
    zev_done = {}
    emitted = set()
    pre_ln2 = [None]

    proj_set = [[("PB", 0), ("PB", 1)]]

    def proj_bank():
        banks = proj_set[0]
        k = banks[pb_rot[0] % len(banks)]
        pb_rot[0] += 1
        return k, bank({"PB": PB, "PC": PC}[k[0]], k[1])

    def emit_prepA(h):
        key = ("prepA", h)
        if key in emitted:
            return
        emitted.add(key)
        wqb = wq_base[h % 2]
        g = None
        for half in (0, 1):
            g = pe_group(
                PD[0:64, half * 512:(half + 1) * 512], [("PD", half)],
                [(ar[:, WKU + (h * 2 + c) * 64: WKU + (h * 2 + c + 1) * 64],
                  ar[:, wqb + c * 1024 + half * 512:
                     wqb + c * 1024 + (half + 1) * 512])
                 for c in (0, 1)],
                waits=[(sWS, ws_all), (sWQ[h % 2], 16 * (h // 2 + 1))])
        q_mm_done[h] = g
        prepA_ev[h] = consume("vector", lambda e: e.tensor_copy(
            ar[0:64, WAB:WAB + 1024], PD[0:64, 0:1024]),
            [("PD", 0), ("PD", 1)], waits=[(sPE, g)], inc=sDVE)

    def emit_prepB(h, on_pa=False):
        key = ("prepB", h)
        if key in emitted:
            return
        emitted.add(key)
        P, bk = (PA, ("PA", 0)) if on_pa else (PD, ("PD", 0))
        w = [(sDVE, prepA_ev[h]), (sPOOL, pool_init)]
        tf = bank_free.get(bk)
        if tf:
            w.append(tf)
        for i in range(8):
            b.emit("tensor", lambda e, i=i, P=P: e.matmul(
                P[:, i * 64:(i + 1) * 64],
                ar[0:64, WAB + i * 128: WAB + (i + 1) * 128],
                ident[0:64, 0:64], start=True, stop=True),
                waits=w if i == 0 else (),
                inc=sPE if i == 7 else None)
        g = sPE.n
        v = consume("vector", lambda e, P=P, h=h: e.tensor_copy(
            ar[:, WAT[h % 2]:WAT[h % 2] + 512], P[:, 0:512]),
            [bk], waits=[(sPE, g)], inc=sDVE)
        wat_done[h % 2] = (sDVE, v)

    def emit_prepC(h):
        key = ("prepC", h)
        if key in emitted:
            return
        emitted.add(key)
        wob = wo_base[h % 2]
        r0 = (h % 2) * 64
        g = None
        for half in (0, 1):
            g = pe_group(
                PD[r0:r0 + 64, half * 512:(half + 1) * 512], [("PD", half)],
                [(ar[:, WVU + (h * 2 + c) * 64: WVU + (h * 2 + c + 1) * 64],
                  ar[:, wob + c * 1024 + half * 512:
                     wob + c * 1024 + (half + 1) * 512])
                 for c in (0, 1)],
                waits=[(sWS, ws_all), (sWO[h % 2], 16 * (h // 2 + 1))])
        wo_mm_done[h] = g
        ms = MSB[(h // 2) % 2]
        v = consume("vector", lambda e, r0=r0, ms=ms: e.tensor_copy(
            ar[r0:r0 + 64, ms:ms + 1024], PD[r0:r0 + 64, 0:1024]),
            [("PD", 0), ("PD", 1)], waits=[(sPE, g)], inc=sDVE)
        msb_done[h % 4] = (sDVE, v)

    def emit_Qp(h, tb, eng="vector"):
        key = ("QP", h, tb)
        if key in emitted:
            return
        emitted.add(key)
        qk, qa = proj_bank()
        g = pe_group(
            qa[0:64, :], [qk],
            [(ar[:, WAT[h % 2] + c * 64: WAT[h % 2] + (c + 1) * 64],
              ar[:, XH + c * T + tb * 512: XH + c * T + (tb + 1) * 512])
             for c in range(NCH)],
            waits=[wat_done[h % 2], (sDVE, xh_attn[tb])])
        qpb = QP + (h % 2) * 2048
        dst = ar[0:64, qpb + tb * 512: qpb + (tb + 1) * 512]
        if eng == "vector":
            v = consume("vector", lambda e, a=qa, d=dst: e.tensor_copy(
                d, a[0:64, :]), [qk], waits=[(sPE, g)], inc=sDVE)
            qpev[h][tb] = (sDVE, v)
        else:
            v = consume("scalar", lambda e, a=qa, d=dst: e.activation(
                d, a[0:64, :], AF.Copy), [qk], waits=[(sPE, g)], inc=sACT)
            qpev[h][tb] = (sACT, v)

    def emit_kv(tb):
        g = pe_group(
            bank(PC, tb % 2)[0:64, :], [("PC", tb % 2)],
            [(ar[:, WKV + c * 64:WKV + (c + 1) * 64],
              ar[:, XH + c * T + tb * 512: XH + c * T + (tb + 1) * 512])
             for c in range(NCH)],
            waits=[(sWS, ws_all), (sDVE, xh_attn[tb])])
        kv_ready_tb[tb] = consume(
            "vector", lambda e, tb=tb: e.tensor_copy(
                ar[0:64, KVA + tb * 512:KVA + (tb + 1) * 512],
                bank(PC, tb % 2)[0:64, :]),
            [("PC", tb % 2)], waits=[(sPE, g)], inc=sDVE)

    def emit_kvt():
        key = ("kvt",)
        if key in emitted:
            return
        emitted.add(key)
        for oct_ in (0, 1):
            P = (PA, PD)[oct_]
            bk = ("PA", 0) if oct_ == 0 else ("PD", 0)
            w = [(sDVE, kv_ready_tb[NTB - 1]), (sPOOL, pool_init)]
            tf = bank_free.get(bk)
            if tf:
                w.append(tf)
            for i in range(8):
                kc = oct_ * 8 + i
                b.emit("tensor", lambda e, kc=kc, i=i, P=P: e.matmul(
                    P[:, i * 64:(i + 1) * 64],
                    ar[0:64, KVA + kc * 128: KVA + (kc + 1) * 128],
                    ident[0:64, 0:64], start=True, stop=True),
                    waits=w if i == 0 else (),
                    inc=sPE if i == 7 else None)
            g = sPE.n
            dst = (ar[:, KVT + oct_ * 8 * 65: KVT + (oct_ * 8 + 8) * 65]
                   .rearrange("p (t c) -> p t c", c=65)[:, :, 0:64])
            sc = P[:, 0:512].rearrange("p (t c) -> p t c", c=64)
            consume("vector", lambda e, d=dst, s=sc: e.tensor_copy(d, s),
                    [bk], waits=[(sPE, g)], inc=sDVE)
        kvt_ready[0] = (sDVE, sDVE.n)

    def emit_Spair(h, j, kp):
        key = ("SP", h, j, kp)
        if key in emitted:
            return
        emitted.add(key)
        bi = s_rot[0] % 2
        s_rot[0] += 1
        P = (PA, PD)[bi]
        keys = [("PA", 0), ("PA", 1)] if bi == 0 else [("PD", 0), ("PD", 1)]
        qpb = QP + (h % 2) * 2048
        g = None
        for hf in (0, 1):
            kc = 2 * kp + hf
            g = pe_group(
                bank(P, hf), [keys[hf]],
                [(ar[:, KVA + kc * 128: KVA + (kc + 1) * 128],
                  ar[:, qpb + j * 512: qpb + (j + 1) * 512])],
                waits=[qpev[h][j], (sDVE, kv_ready_tb[kp // 2]),
                       (sPOOL, pool_qp)])
        pbuf = RA + (j % 2) * 8192
        ew = [(sPE, g)]
        if h == 0 and j <= 1:
            ew.append((sDVE, xh_attn[1 if kp % 2 == 0 else NTB - 1]))
        v = b.emit(
            "scalar", lambda e, P=P, p=pbuf, kp=kp: e.activation(
                ar[:, p + kp * 1024: p + (kp + 1) * 1024], P[:, 0:1024],
                AF.Exp, scale=SC_ATTN),
            waits=ew, inc=sACT)
        exp_done[(h, j)] = v
        bank_free[keys[0]] = (sACT, v)
        bank_free[keys[1]] = (sACT, v)

    def emit_PVg(h, j, qs):
        pbuf = RA + (j % 2) * 8192
        zk = ("PC", qs % 2)
        za = bank(PC, qs % 2)
        g = pe_group(
            za[:, 0:65], [zk],
            [(ar[:, pbuf + kc * 512 + qs * 128: pbuf + kc * 512 + (qs + 1) * 128],
              ar[:, KVT + kc * 65: KVT + (kc + 1) * 65])
             for kc in range(NKC)],
            waits=[(sACT, exp_done[(h, j)]), kvt_ready[0]])
        b.emit("vector", lambda e, a=za: e.tensor_copy(t1[:, 0:1], a[:, 64:65]),
               waits=[(sPE, g)], inc=sDVE)
        b.emit("vector", lambda e: e.reciprocal(rc, t1[:, 0:1]), inc=sDVE)
        tc = j * 4 + qs
        zev_done[(h, j, qs)] = consume(
            "vector", lambda e, a=za, tc=tc: e.tensor_scalar_mul(
                ar[:, UTM + tc * 64: UTM + (tc + 1) * 64], a[:, 0:64], rc),
            [zk], waits=(), inc=sDVE)

    def emit_UT(h, oct_):
        key = ("UT", h, oct_)
        if key in emitted:
            return
        emitted.add(key)
        P = PB
        keys = [("PB", 0), ("PB", 1)]
        w = [(sDVE, zev_done[(h, 2 * oct_ + 1, 3)]), (sPOOL, pool_init)]
        for k in keys:
            tf = bank_free.get(k)
            if tf:
                w.append(tf)
        r0 = (h % 2) * 64
        for i in range(8):
            tc = oct_ * 8 + i
            b.emit("tensor", lambda e, tc=tc, i=i, P=P, r0=r0: e.matmul(
                P[r0:r0 + 64, i * 128:(i + 1) * 128],
                ar[:, UTM + tc * 64: UTM + (tc + 1) * 64],
                ident, start=True, stop=True),
                waits=w if i == 0 else (),
                inc=sPE if i == 7 else None)
        g = sPE.n
        ub = UFM[(h // 2) % 2]
        dst = ar[r0:r0 + 64, ub + oct_ * 1024: ub + (oct_ + 1) * 1024]
        v = consume("scalar", lambda e, P=P, d=dst, r0=r0: e.activation(
            d, P[r0:r0 + 64, 0:1024], AF.Copy),
            keys, waits=[(sPE, g)], inc=sACT)
        ut_done[h % 2][oct_] = (sACT, v)

    def emit_WO2(h1, o, oct_):
        P, pk = ((PB, "PB"), (PC, "PC"))[wo_pair[0] % 2]
        wo_pair[0] += 1
        h0 = h1 - 1
        ps = (h1 // 2) % 2
        g = None
        for n in (0, 1):
            tb = oct_ * 2 + n
            g = pe_group(
                bank(P, n), [(pk, n)],
                [(ar[:, MSB[ps] + o * 128: MSB[ps] + (o + 1) * 128],
                  ar[:, UFM[ps] + tb * 512: UFM[ps] + (tb + 1) * 512])],
                waits=[msb_done[h0 % 4], msb_done[h1 % 4],
                       ut_done[0][oct_], ut_done[1][oct_]])
        wo_add_last[o] = consume(
            "vector", lambda e, P=P, o=o, oc=oct_: e.tensor_add(
                x[:, o * T + oc * 1024: o * T + (oc + 1) * 1024], P[:, 0:1024],
                x[:, o * T + oc * 1024: o * T + (oc + 1) * 1024]),
            [(pk, 0), (pk, 1)],
            waits=[(sPE, g)] + [(s, 16) for s in sX], inc=sDVE)

    def emit_ln(first, out_ready, pre=None, after_tb=None, pre_tb=None):
        xb_tb = [None] * NTB
        xsq_tb = [None] * NTB

        def emit_pre(tb):
            if pre is not None:
                xb_tb[tb] = list(pre[0][tb])
                xsq_tb[tb] = list(pre[1][tb])
                return
            x3 = (x.rearrange("p (c n) -> p c n", c=NCH)
                  [:, :, tb * 512:(tb + 1) * 512])
            xb3 = (ar[:, RA:RA + NCH * T].rearrange("p (c n) -> p c n", c=NCH)
                   [:, :, tb * 512:(tb + 1) * 512])
            xq3 = (ar[:, XH:XH + NCH * T].rearrange("p (c n) -> p c n", c=NCH)
                   [:, :, tb * 512:(tb + 1) * 512])
            cv = b.emit("scalar", lambda e, o=xb3, i=x3: e.activation(
                o, i, AF.Copy), waits=[(sX[tb], 16)], inc=sACT)
            sv = b.emit("scalar", lambda e, o=xq3, i=x3: e.activation(
                o, i, AF.Square), inc=sACT)
            xb_tb[tb] = [(sACT, cv)]
            xsq_tb[tb] = [(sACT, sv)]

        emit_pre(0)
        emit_pre(1)

        srow = [(PA[0:1, 0:512], PA[0:1, 512:1024], ("PA", 0), ("PA", 1)),
                (PD[0:1, 0:512], PD[0:1, 512:1024], ("PD", 0), ("PD", 1))]
        stats_done = [None] * NTB

        def emit_stats(tb):
            r0, r1, k0, k1 = srow[tb % 2]
            sum_done = pe_group(
                r0, [k0],
                [(ones_bf, ar[:, RA + c * T + tb * 512: RA + c * T + (tb + 1) * 512])
                 for c in range(NCH)],
                waits=list(xb_tb[tb]) + [(sPOOL, pool_init)])
            sumsq_done = pe_group(
                r1, [k1],
                [(ones_bf, ar[:, XH + c * T + tb * 512: XH + c * T + (tb + 1) * 512])
                 for c in range(NCH)],
                waits=list(xsq_tb[tb]))
            stats_done[tb] = (sum_done, sumsq_done)

        r0a, r1a, k0a, k1a = srow[0]
        r0b, r1b, k0b, k1b = srow[1]
        sum0 = pe_group(r0a, [k0a],
                        [(ones_bf, ar[:, RA + c * T: RA + c * T + 512])
                         for c in range(NCH)],
                        waits=list(xb_tb[0]) + [(sPOOL, pool_init)])
        sq0 = pe_group(r1a, [k1a],
                       [(ones_bf, ar[:, XH + c * T: XH + c * T + 512])
                        for c in range(NCH)],
                       waits=list(xsq_tb[0]))
        sum1 = pe_group(r0b, [k0b],
                        [(ones_bf, ar[:, RA + c * T + 512: RA + c * T + 1024])
                         for c in range(NCH)],
                        waits=list(xb_tb[1]))
        sq1 = pe_group(r1b, [k1b],
                       [(ones_bf, ar[:, XH + c * T + 512: XH + c * T + 1024])
                        for c in range(NCH)],
                       waits=list(xsq_tb[1]))
        stats_done[0] = (sum0, sq0)
        stats_done[1] = (sum1, sq1)

        for tb in range(NTB):
            r0, r1, k0, k1 = srow[tb % 2]
            ro = (tb % 2) * 1536
            bo = (tb % 2) * 512
            mbcs = mbc[:, bo:bo + 512]
            rbcs = rbc[:, bo:bo + 512]
            sum_done, sumsq_done = stats_done[tb]
            v_mean = b.emit(
                "vector", lambda e, r0=r0, ro=ro: e.tensor_scalar_mul(
                    rows[0:1, ro:ro + 512], r0, 1.0 / 1024.0),
                waits=[(sPE, sum_done)], inc=sDVE)
            b.emit("vector", lambda e, r0=r0, ro=ro: e.tensor_mul(
                rows[0:1, ro + 1024:ro + 1536], rows[0:1, ro:ro + 512], r0),
                inc=sDVE)
            v_var = b.emit("vector", lambda e, r1=r1, ro=ro: e.tensor_sub(
                rows[0:1, ro + 1024:ro + 1536], r1,
                rows[0:1, ro + 1024:ro + 1536]),
                waits=[(sPE, sumsq_done)], inc=sDVE)
            bank_free[k0] = (sDVE, v_var)
            bank_free[k1] = (sDVE, v_var)
            if pre_tb is not None:
                pre_tb(tb)
            v_vr = b.emit("vector", lambda e, ro=ro: e.tensor_scalar(
                rows[0:1, ro + 512:ro + 1024], rows[0:1, ro + 1024:ro + 1536],
                1.0 / 1024.0, EPS, ALU.mult, ALU.add), inc=sDVE)
            a_ln = b.emit("scalar", lambda e, ro=ro: e.activation(
                rows[0:1, ro + 1024:ro + 1536], rows[0:1, ro + 512:ro + 1024],
                AF.Ln), waits=[(sDVE, v_vr)], inc=sACT)
            a_rstd = b.emit("scalar", lambda e, ro=ro: e.activation(
                rows[0:1, ro + 512:ro + 1024], rows[0:1, ro + 1024:ro + 1536],
                AF.Exp, scale=-0.5), waits=[(sACT, a_ln)], inc=sACT)
            vm_b = b.emit("vector", lambda e, ro=ro, m=mbcs: e.tensor_copy(
                m[0:1, :], rows[0:1, ro:ro + 512]),
                waits=[(sDVE, v_mean)], inc=sDVE)
            vr_b = b.emit("vector", lambda e, ro=ro, r=rbcs: e.tensor_copy(
                r[0:1, :], rows[0:1, ro + 512:ro + 1024]),
                waits=[(sACT, a_rstd)], inc=sDVE)
            bc_m = pe_group(PB[:, 0:512], [("PB", 0)],
                            [(onesb_row, mbcs[0:1, :])],
                            waits=[(sDVE, vm_b), (sPOOL, pool_init)])
            bc_r = pe_group(PB[:, 512:1024], [("PB", 1)],
                            [(onesb_row, rbcs[0:1, :])],
                            waits=[(sDVE, vr_b)])
            if first:
                cm = consume("vector", lambda e, m=mbcs: e.tensor_copy(
                    m, PB[:, 0:512]), [("PB", 0)],
                    waits=[(sPE, bc_m)], inc=sDVE)
                cr = consume("vector", lambda e, r=rbcs: e.tensor_copy(
                    r, PB[:, 512:1024]), [("PB", 1)],
                    waits=[(sPE, bc_r)], inc=sDVE)
                crw = (sDVE, cr)
            else:
                cm = consume("scalar", lambda e, m=mbcs: e.activation(
                    m, PB[:, 0:512], AF.Copy), [("PB", 0)],
                    waits=[(sPE, bc_m)], inc=sACT)
                cr = consume("scalar", lambda e, r=rbcs: e.activation(
                    r, PB[:, 512:1024], AF.Copy), [("PB", 1)],
                    waits=[(sPE, bc_r)], inc=sACT)
                crw = (sACT, cr)
            xh3 = (ar[:, XH:XH + NCH * T].rearrange("p (c n) -> p c n", c=NCH)
                   [:, :, tb * 512:(tb + 1) * 512])
            xb3 = (ar[:, RA:RA + NCH * T].rearrange("p (c n) -> p c n", c=NCH)
                   [:, :, tb * 512:(tb + 1) * 512])
            mb3 = mbcs.unsqueeze(1).broadcast_to([128, NCH, 512])
            rb3 = rbcs.unsqueeze(1).broadcast_to([128, NCH, 512])
            b.emit("vector", lambda e, o=xh3, a=xb3, m=mb3: e.tensor_sub(o, a, m),
                   waits=[crw], inc=sDVE)
            vv = b.emit("vector", lambda e, o=xh3, r=rb3: e.tensor_mul(o, o, r),
                        inc=sDVE)
            out_ready.append(vv)
            if after_tb is not None:
                after_tb(tb)
            if tb + 2 < NTB:
                emit_pre(tb + 2)
                emit_stats(tb + 2)
        return out_ready

    def ln1_pre(tb):
        if tb == 1:
            emit_kv(0)
            emit_prepA(0)
        elif tb == 2:
            emit_kv(1)
            emit_prepB(0, on_pa=True)
            emit_Qp(0, 0)
        elif tb == 3:
            emit_Spair(0, 0, 0)
            emit_Spair(0, 0, 2)
            emit_kv(2)
            emit_prepC(0)

    def ln1_after(tb):
        if tb == 3:
            emit_kv(3)
            emit_Spair(0, 0, 4)
            emit_kvt()

    emit_ln(True, xh_attn, after_tb=ln1_after, pre_tb=ln1_pre)
    maybe_tap("xh1", ar[:, XH:XH + NCH * T], [(sDVE, xh_attn[-1])])
    maybe_tap("kv", ar[0:64, KVA:KVA + T], [(sDVE, kv_ready_tb[-1])])
    maybe_tap("wab0", ar[0:64, WAB:WAB + 1024], [(sDVE, prepA_ev[0])])

    for h in range(H):
        nxt = h + 1
        emit_prepA(h)
        emit_prepB(h)
        emit_prepC(h)
        if h > 0:
            for tb in range(NTB):
                emit_Qp(h, tb)
        for kp in range(8):
            emit_Spair(h, 0, kp)
            if h == 0 and kp < 3:
                emit_Qp(0, kp + 1)
        if h == 0:
            maybe_tap("qp0", ar[0:64, QP:QP + 2048], [qpev[0][NTB - 1]])
            maybe_tap("p0", ar[:, RA:RA + 8192], [(sACT, exp_done[(0, 0)])])
        fill = {}
        if nxt < H:
            fill = {(1, 1): [lambda: emit_prepA(nxt)],
                    (2, 1): [lambda: emit_prepB(nxt)],
                    (2, 3): [lambda: emit_Qp(nxt, 0)],
                    (3, 0): [lambda: emit_Qp(nxt, 1)],
                    (3, 1): [lambda: emit_prepC(nxt)],
                    (3, 2): [lambda: emit_Qp(nxt, 2)],
                    (3, 3): [lambda: emit_Qp(nxt, 3)]}
        wu = []
        if h % 2 == 0 and h > 0:
            wu = [(h - 1, o, oc) for o in range(4) for oc in (0, 1)]
        elif h % 2 == 1 and h >= 3:
            wu = [(h - 2, o, oc) for o in range(4, NCH) for oc in (0, 1)]
        for idx, (h1, o, oc) in enumerate(wu):
            slot = (1 + idx // 4, idx % 4)
            fill.setdefault(slot, []).insert(0, (lambda h1=h1, o=o, oc=oc:
                                                 emit_WO2(h1, o, oc)))
        for j in (1, 2, 3):
            for i in range(4):
                emit_Spair(h, j, 2 * i)
                emit_Spair(h, j, 2 * i + 1)
                emit_PVg(h, j - 1, i)
                for f in fill.get((j, i), ()):
                    f()
        if h + 2 < H:
            load_wq(h + 2, extra_waits=[(sPE, q_mm_done[h])])
            load_wo(h + 2, extra_waits=[(sPE, wo_mm_done[h])])

        emit_PVg(h, 3, 0)
        if nxt < H:
            emit_Spair(nxt, 0, 0)
        emit_PVg(h, 3, 1)
        if nxt < H:
            emit_Spair(nxt, 0, 1)
        emit_PVg(h, 3, 2)
        if nxt < H:
            emit_Spair(nxt, 0, 2)
        emit_PVg(h, 3, 3)
        if nxt < H:
            emit_Spair(nxt, 0, 3)
        emit_UT(h, 0)
        if nxt < H:
            emit_Spair(nxt, 0, 4)
        emit_UT(h, 1)
        if nxt < H:
            emit_Spair(nxt, 0, 5)
            emit_Spair(nxt, 0, 6)
            emit_Spair(nxt, 0, 7)
            if h % 2 == 1:
                emit_Spair(nxt, 1, 0)
                emit_Spair(nxt, 1, 1)
                emit_Spair(nxt, 1, 2)
                emit_Spair(nxt, 1, 3)
        else:
            xb_l = [None] * NTB
            xsq_l = [None] * NTB
            for oct_ in (0, 1):
                for o in range(NCH):
                    emit_WO2(h, o, oct_)
                av = wo_add_last[NCH - 1]
                sh = (sXB, sXB2)[oct_]
                x3h = (x.rearrange("p (c n) -> p c n", c=NCH)
                       [:, :, oct_ * 1024:(oct_ + 1) * 1024])
                xb3h = (ar[:, RA:RA + NCH * T]
                        .rearrange("p (c n) -> p c n", c=NCH)
                        [:, :, oct_ * 1024:(oct_ + 1) * 1024])
                b.emit("gpsimd", lambda e, o=xb3h, i=x3h: e.dma_start(
                    out=o, in_=i),
                    waits=[(sDVE, av)], inc=sh, selfsync=False)
                for tb in (oct_ * 2, oct_ * 2 + 1):
                    x3 = (x.rearrange("p (c n) -> p c n", c=NCH)
                          [:, :, tb * 512:(tb + 1) * 512])
                    xq3 = (ar[:, XH:XH + NCH * T]
                           .rearrange("p (c n) -> p c n", c=NCH)
                           [:, :, tb * 512:(tb + 1) * 512])
                    sv = b.emit("scalar", lambda e, o=xq3, i=x3: e.activation(
                        o, i, AF.Square), waits=[(sDVE, av)], inc=sACT)
                    xb_l[tb] = [(sh, 16)]
                    xsq_l[tb] = [(sACT, sv)]
            pre_ln2[0] = (xb_l, xsq_l)
            wo_mm_done[H - 1] = sPE.n

        if h == 0:
            maybe_tap("u0", ar[:, UTM:UTM + 1024], [(sDVE, zev_done[(0, 3, 3)])])
            maybe_tap("uf0", ar[:, UFM[0]:UFM[0] + 2048], [ut_done[0][1]])

    pe_attn_done = sPE.n
    maybe_tap("xa", x[:, 0:NCH * T], [(sDVE, max(wo_add_last))])

    wg_load_done = [None] * NFC
    gate_done = [None] * NFC
    up_done = [None] * NFC
    hmul_done = [[None, None] for _ in range(NFC)]

    def load_wg(c, waits):
        b.emit("gpsimd", lambda e, c=c: e.dma_start(
            out=ar[:, WG[c % 2]:WG[c % 2] + 1024], in_=wg_d[c]),
            waits=waits, inc=sWG[c % 2], selfsync=False)

    def load_wu(c, waits):
        b.emit("gpsimd", lambda e, c=c: e.dma_start(
            out=ar[:, WU[c % 2]:WU[c % 2] + 1024], in_=wu_d[c]),
            waits=waits, inc=sWU[c % 2], selfsync=False)

    def load_wd(c, waits):
        ci = c % NHALF
        b.emit("gpsimd", lambda e, c=c, ci=ci: e.dma_start(
            out=ar[:, WDB + ci * 1024:WDB + (ci + 1) * 1024],
            in_=wd_d[c]),
            waits=waits, inc=sWD, selfsync=False)

    load_wg(0, [(sPE, q_mm_done[6])])
    load_wg(1, [])
    load_wu(0, [(sPE, q_mm_done[7])])
    load_wu(1, [])
    for c in range(NHALF):
        load_wd(c, [(sPE, wo_mm_done[7])] if c == 0 else [])

    G_ap = [PA, PC]
    U_ap = [PB, PD]
    G_keys = [[("PA", 0), ("PA", 1)], [("PC", 0), ("PC", 1)]]
    U_keys = [[("PB", 0), ("PB", 1)], [("PD", 0), ("PD", 1)]]

    scr_free = [None]
    down_b1_done = [0]

    def emit_ffn_A(c, t):
        key = ("A", c, t)
        if key in emitted:
            return
        emitted.add(key)
        half = c // NHALF
        ci = c % NHALF
        gg = None
        for n in range(2):
            tb = t * 2 + n
            gg = pe_group(
                bank(G_ap[t], n), [G_keys[t][n]],
                [(ar[:, WG[c % 2] + k * 128: WG[c % 2] + (k + 1) * 128],
                  ar[:, XH + k * T + tb * 512: XH + k * T + (tb + 1) * 512])
                 for k in range(NCH)],
                waits=[(sWG[c % 2], 16 * (c // 2 + 1)), (sDVE, xh_ffn[tb])])
        gate_done[c] = max(gate_done[c] or 0, gg)
        ug = None
        for n in range(2):
            tb = t * 2 + n
            ug = pe_group(
                bank(U_ap[t], n), [U_keys[t][n]],
                [(ar[:, WU[c % 2] + k * 128: WU[c % 2] + (k + 1) * 128],
                  ar[:, XH + k * T + tb * 512: XH + k * T + (tb + 1) * 512])
                 for k in range(NCH)],
                waits=[(sWU[c % 2], 16 * (c // 2 + 1)), (sDVE, xh_ffn[tb])])
        up_done[c] = max(up_done[c] or 0, ug)
        scr_w = [(sPE, wo_mm_done[6])] if scr_free[0] is None else [scr_free[0]]
        a_si = consume(
            "scalar", lambda e, t=t: e.activation(
                ar[:, SCR_S:SCR_S + 1024], G_ap[t][:, 0:1024], AF.Silu,
                scale=INV127),
            G_keys[t], waits=[(sPE, gg)] + scr_w, inc=sACT)
        a_up = consume(
            "scalar", lambda e, t=t: e.activation(
                ar[:, SCR_U:SCR_U + 1024], U_ap[t][:, 0:1024], AF.Copy,
                scale=INV127 * INV127),
            U_keys[t], waits=[(sPE, ug)], inc=sACT)
        hcol = HHB + ci * T + t * 1024
        hw = [(sACT, a_up)]
        if half == 1:
            hw.append((sPE, down_b1_done[0]))
        hmul_done[c][t] = b.emit(
            "vector", lambda e, hcol=hcol: e.tensor_mul(
                ar[:, hcol:hcol + 1024],
                ar[:, SCR_S:SCR_S + 1024], ar[:, SCR_U:SCR_U + 1024]),
            waits=hw, inc=sDVE)
        scr_free[0] = (sDVE, hmul_done[c][t])
        if t == 1 and c + 2 < NFC:
            load_wg(c + 2, [(sPE, gate_done[c])])
            load_wu(c + 2, [(sPE, up_done[c])])

    def ln2_pre(tb):
        if tb == 2:
            emit_ffn_A(0, 0)
        elif tb == 3:
            emit_ffn_A(1, 0)

    emit_ln(False, xh_ffn, pre=pre_ln2[0], pre_tb=ln2_pre)
    maybe_tap("xh2", ar[:, XH:XH + NCH * T], [(sDVE, xh_ffn[-1])])

    down_add_final = None
    for half in range(2):
        c0 = half * NHALF
        order = [(0, 0), (1, 0), (0, 1), (1, 1)]
        order += [(ci, t) for ci in range(2, NHALF) for t in range(2)]
        for (ci, t) in order:
            emit_ffn_A(c0 + ci, t)
        if half == 0:
            maybe_tap("h0", ar[:, HHB:HHB + NHALF * T],
                      [(sDVE, hmul_done[NHALF - 1][1])])

        d_rot = 0
        for o in range(NCH):
            for tp in range(2):
                P, pk = ((PA, "PA"), (PB, "PB"))[d_rot % 2]
                d_rot += 1
                g = None
                for n in (0, 1):
                    tb = tp * 2 + n
                    g = pe_group(
                        bank(P, n), [(pk, n)],
                        [(ar[:, WDB + ci * 1024 + o * 128: WDB + ci * 1024 + (o + 1) * 128],
                          ar[:, HHB + ci * T + tb * 512: HHB + ci * T + (tb + 1) * 512])
                         for ci in range(NHALF)],
                        waits=[(sWD, 16 * NHALF * (half + 1)),
                               (sDVE, hmul_done[c0 + NHALF - 1][tp])])
                down_add_final = consume(
                    "vector", lambda e, P=P, o=o, tp=tp: e.tensor_add(
                        x[:, o * T + tp * 1024: o * T + (tp + 1) * 1024], P[:, 0:1024],
                        x[:, o * T + tp * 1024: o * T + (tp + 1) * 1024]),
                    [(pk, 0), (pk, 1)], waits=[(sPE, g)], inc=sDVE)
            if half == 1:
                b.emit("sync", lambda e, o=o: e.dma_start(
                    out=out_d[o * 128:(o + 1) * 128, :],
                    in_=x[:, o * T:(o + 1) * T]),
                    waits=[(sDVE, down_add_final)], inc=sST)
        down_b1_done[0] = sPE.n
        if half == 0:
            for c in range(NHALF, NFC):
                load_wd(c, [(sPE, down_b1_done[0])] if c == NHALF else [])

    fin = sST.n
    b.emit("sync", lambda e: e.wait_ge(sST.h, fin))

    with nc.Block() as block:
        @block.tensor
        def _(eng):
            for f in b.q["tensor"]:
                f(eng)

        @block.scalar
        def _(eng):
            for f in b.q["scalar"]:
                f(eng)

        @block.vector
        def _(eng):
            for f in b.q["vector"]:
                f(eng)

        @block.gpsimd
        def _(eng):
            for f in b.q["gpsimd"]:
                f(eng)

        @block.sync
        def _(eng):
            for f in b.q["sync"]:
                f(eng)

    return nc


def _prep_inputs(L0_attn_x, L0_attn_Wq, L0_attn_Wkv_down, L0_attn_Wk_up,
                 L0_attn_Wv_up, L0_attn_Wo, L0_ffn_w_gate, L0_ffn_w_up,
                 L0_ffn_w_down):
    xs = np.asarray(L0_attn_x)
    Wq = np.asarray(L0_attn_Wq)
    Wkv = np.asarray(L0_attn_Wkv_down)
    Wku = np.asarray(L0_attn_Wk_up)
    Wvu = np.asarray(L0_attn_Wv_up)
    Wo = np.asarray(L0_attn_Wo)
    Wg = np.asarray(L0_ffn_w_gate)
    Wu = np.asarray(L0_ffn_w_up)
    Wd = np.asarray(L0_ffn_w_down)

    wq = np.ascontiguousarray(
        Wq.reshape(H, 2, 128, D).transpose(0, 2, 1, 3).reshape(H, 128, 2048))
    wkv = np.ascontiguousarray(
        Wkv.T.reshape(NCH, 128, LR).transpose(1, 0, 2).reshape(128, 512))
    wku = np.ascontiguousarray(
        Wku.reshape(H, 2, 128, LR).transpose(2, 0, 1, 3).reshape(128, 1024))
    wvu = np.ascontiguousarray(
        Wvu.reshape(H, 2, 128, LR).transpose(2, 0, 1, 3).reshape(128, 1024))
    wo = np.ascontiguousarray(
        Wo.T.reshape(H, 2, 128, D).transpose(0, 2, 1, 3).reshape(H, 128, 2048))
    wg = np.ascontiguousarray(
        Wg.T.reshape(NCH, 128, NFC, 128).transpose(2, 1, 0, 3).reshape(NFC, 128, 1024))
    wu = np.ascontiguousarray(
        Wu.T.reshape(NCH, 128, NFC, 128).transpose(2, 1, 0, 3).reshape(NFC, 128, 1024))
    wd = np.ascontiguousarray(Wd.T.reshape(NFC, 128, D))
    maps = []
    for i in range(xs.shape[0]):
        xq = np.ascontiguousarray(
            xs[i].T.reshape(NCH, 128, NTB, 512).transpose(2, 1, 0, 3))
        maps.append(dict(
            x=xq, wq=wq, wkv=wkv, wku=wku,
            wvu=wvu, wo=wo, wg=wg, wu=wu, wd=wd))
    return maps


def kernel(L0_attn_x, L0_attn_Wq, L0_attn_Wkv_down, L0_attn_Wk_up,
           L0_attn_Wv_up, L0_attn_Wo, L0_ffn_w_gate, L0_ffn_w_up,
           L0_ffn_w_down):
    maps = _prep_inputs(L0_attn_x, L0_attn_Wq, L0_attn_Wkv_down, L0_attn_Wk_up,
                        L0_attn_Wv_up, L0_attn_Wo, L0_ffn_w_gate, L0_ffn_w_up,
                        L0_ffn_w_down)
    nc = build_program()
    res = run_bass_kernel_spmd(nc, maps, core_ids=list(range(8)))
    out = np.stack([np.asarray(r["out"]).T for r in res.results])
    return np.ascontiguousarray(out.astype(np.float32))


# version 44
# speedup vs baseline: 1.0043x; 1.0043x over previous
import numpy as np

import concourse.bass as bass
import concourse.mybir as mybir
from concourse.bass_utils import run_bass_kernel_spmd

F32 = mybir.dt.float32
BF16 = mybir.dt.bfloat16
I32 = mybir.dt.int32
AF = mybir.ActivationFunctionType
ALU = mybir.AluOpType

T = 2048
D = 1024
H = 8
DH = 256
LR = 64
FH = 3072
NCH = 8
NTB = 4
NKC = 16
NFC = 24
NHALF = 12
INV127 = 1.0 / 127.0
SC_ATTN = 1.0 / 16.0
EPS = 1e-5

XH = 0
RA = 16384
QH = 32768
KH = 36864
VH = 40960
ZT = 45072
ZF = 49168
WQ0, WQ1 = 53264, 55312
WO0, WO1 = 57360, 59408
WKU = 43008
WVU = 44048
WKV = 50192
KVA = 50704
IDC = 52752
ONE = 53248
ARENA = 61456
QP = QH
KVT = KH
WAB = KH + 1040
WAT = (KH + 2064, KH + 2576)
MSB = (VH, VH + 1024)
UTM = 45072
UFM = (46096, 48144)
HHB = RA
WDB = VH
WG = (WQ0, WQ0 + 1024)
WU = (WQ1, WQ1 + 1024)
SCR_S = WO0
SCR_U = WO0 + 1024


class _Sem:
    __slots__ = ("h", "step", "n")


class _Builder:
    def __init__(self, nc):
        self.nc = nc
        self.q = {"tensor": [], "scalar": [], "vector": [], "gpsimd": [], "sync": []}
        self.waited = {k: {} for k in self.q}
        self.selfsem = {}

    def sem(self, name, step=1):
        s = _Sem()
        s.h = self.nc.alloc_semaphore(name)
        s.step = step
        s.n = 0
        return s

    def emit(self, eng, fn, waits=(), inc=None, selfsync=True):
        ss = self.selfsem.get(eng)
        if ss is not None and selfsync:
            waits = list(waits) + ([(ss, ss.n)] if ss.n > 0 else [])
            if inc is None:
                inc = ss
            else:
                assert inc is ss, f"{eng} compute ops must inc the engine sem"
        wl = []
        for (s, v) in waits:
            if s is None or v is None or v <= 0:
                continue
            assert v <= s.n, f"wait beyond emitted count on {eng}: {v} > {s.n}"
            w = self.waited[eng]
            k = id(s)
            if v > w.get(k, 0):
                w[k] = v
                wl.append((s.h, v))
        after = None
        if inc is not None:
            inc.n += inc.step
            after = (inc.h, inc.step)

        def run(e, wl=wl, fn=fn, after=after):
            for (hh, v) in wl:
                e.wait_ge(hh, v)
            inst = fn(e)
            if after is not None:
                inst.then_inc(after[0], after[1])

        self.q[eng].append(run)
        return inc.n if inc is not None else None


def build_program(tap=None):
    nc = bass.Bass("TRN2", target_bir_lowering=False)

    x_d = nc.dram_tensor("x", [NTB, 128, NCH, 512], F32, kind="ExternalInput").ap()
    wq_d = nc.dram_tensor("wq", [H, 128, 2048], F32, kind="ExternalInput").ap()
    wkv_d = nc.dram_tensor("wkv", [128, 512], F32, kind="ExternalInput").ap()
    wku_d = nc.dram_tensor("wku", [128, 1024], F32, kind="ExternalInput").ap()
    wvu_d = nc.dram_tensor("wvu", [128, 1024], F32, kind="ExternalInput").ap()
    wo_d = nc.dram_tensor("wo", [H, 128, 2048], F32, kind="ExternalInput").ap()
    wg_d = nc.dram_tensor("wg", [NFC, 128, 1024], I32, kind="ExternalInput").ap()
    wu_d = nc.dram_tensor("wu", [NFC, 128, 1024], I32, kind="ExternalInput").ap()
    wd_d = nc.dram_tensor("wd", [NFC, 128, D], I32, kind="ExternalInput").ap()
    out_d = nc.dram_tensor("out", [D, T], F32, kind="ExternalOutput").ap()

    x_t = nc.alloc_sbuf_tensor("x_fm", [128, NCH * T], F32)
    ar_t = nc.alloc_sbuf_tensor("arena", [128, ARENA], BF16)
    t1_t = nc.alloc_sbuf_tensor("t1", [128, 16], F32)
    rows_t = nc.alloc_sbuf_tensor("rows", [1, 3072], F32)
    rc_t = nc.alloc_sbuf_tensor("rc", [128, 1], F32)
    mbc_t = nc.alloc_sbuf_tensor("mbc", [128, 1024], BF16)
    rbc_t = nc.alloc_sbuf_tensor("rbc", [128, 1024], BF16)
    onesb_row_t = nc.alloc_sbuf_tensor("onesb_row", [1, 128], BF16)
    x = x_t.ap()
    ar = ar_t.ap()
    t1 = t1_t.ap()
    rows = rows_t.ap()
    rc = rc_t.ap()
    mbc = mbc_t.ap()
    rbc = rbc_t.ap()
    onesb_row = onesb_row_t.ap()

    PA = nc.alloc_psum_tensor("PA", [128, 1024], F32).ap()
    PB = nc.alloc_psum_tensor("PB", [128, 1024], F32).ap()
    PC = nc.alloc_psum_tensor("PC", [128, 1024], F32).ap()
    PD = nc.alloc_psum_tensor("PD", [128, 1024], F32).ap()

    def bank(P, i):
        return P[:, i * 512:(i + 1) * 512]

    b = _Builder(nc)
    sX = [b.sem(f"sX{i}", 16) for i in range(4)]
    sWS = b.sem("sWS", 16)
    sXB = b.sem("sXB", 16)
    sXB2 = b.sem("sXB2", 16)
    sWQ = [b.sem("sWQ0", 16), b.sem("sWQ1", 16)]
    sWO = [b.sem("sWO0", 16), b.sem("sWO1", 16)]
    sWG = [b.sem("sWG0", 16), b.sem("sWG1", 16)]
    sWU = [b.sem("sWU0", 16), b.sem("sWU1", 16)]
    sWD = b.sem("sWD", 16)
    sST = b.sem("sST", 16)
    sPE = b.sem("sPE", 1)
    sACT = b.sem("sACT", 1)
    sDVE = b.sem("sDVE", 1)
    sPOOL = b.sem("sPOOL", 1)
    b.selfsem = {"vector": sDVE, "scalar": sACT, "gpsimd": sPOOL}

    bank_free = {}

    def pe_group(out_ap, keys, pieces, waits=()):
        w = list(waits)
        for k in keys:
            bf = bank_free.get(k)
            if bf:
                w.append(bf)
        n = len(pieces)
        for i, (l, r) in enumerate(pieces):
            b.emit(
                "tensor",
                (lambda e, o=out_ap, l=l, r=r, st=(i == 0), sp=(i == n - 1):
                 e.matmul(o, l, r, start=st, stop=sp)),
                waits=w if i == 0 else (),
                inc=sPE if i == n - 1 else None,
            )
        return sPE.n

    def consume(eng, fn, keys, waits, inc):
        v = b.emit(eng, fn, waits=waits, inc=inc)
        for k in keys:
            bank_free[k] = (inc, v)
        return v

    def maybe_tap(name, src_ap, waits):
        if tap == name:
            td = nc.dram_tensor("tap", list(src_ap.shape), src_ap.dtype,
                                kind="ExternalOutput").ap()
            b.emit("sync", lambda e, s=src_ap, t=td: e.dma_start(out=t, in_=s),
                   waits=waits, inc=sST)

    wz = b.emit("gpsimd", lambda e: e.memset(ar[:, QH:QH + 512], 0.0), inc=sPOOL)
    ident = ar[:, IDC:IDC + 128]
    ones_bf = ar[:, ONE:ONE + 1]
    b.emit("gpsimd", lambda e: e.memset(ident, 0.0), inc=sPOOL)
    b.emit("gpsimd", lambda e: e.affine_select(
        out=ident, in_=ident,
        compare_op=ALU.not_equal, fill=1.0, base=0,
        pattern=[[-1, 128]], channel_multiplier=1), inc=sPOOL)
    b.emit("gpsimd", lambda e: e.memset(ones_bf, 1.0), inc=sPOOL)
    b.emit("gpsimd", lambda e: e.memset(onesb_row, 1.0), inc=sPOOL)
    kvt1 = ar[:, KVT:KVT + NKC * 65].rearrange("p (t c) -> p t c", c=65)[:, :, 64:65]
    b.emit("gpsimd", lambda e: e.memset(kvt1, 1.0), inc=sPOOL)
    b.emit("gpsimd", lambda e: e.memset(ar[64:128, KVA:KVA + 2048], 0.0), inc=sPOOL)
    pool_init = sPOOL.n
    for tb in range(NTB):
        b.emit("sync", lambda e, tb=tb: e.dma_start(
            out=x.rearrange("p (c n) -> p c n", c=NCH)
                [:, :, tb * 512:(tb + 1) * 512],
            in_=x_d[tb]),
            inc=sX[tb])
    for i in range(63):
        b.emit("tensor", lambda e: e.matmul(
            PD[:, 0:512], ar[:, QH:QH + 128], ar[:, QH:QH + 512],
            start=True, stop=True),
            waits=[(sPOOL, wz)] if i == 0 else (), inc=None)
    b.emit("tensor", lambda e: e.matmul(
        PD[:, 0:512], ar[:, QH:QH + 128], ar[:, QH:QH + 512],
        start=True, stop=True), inc=sPE)
    warm_done = sPE.n
    bank_free[("PD", 0)] = (sPE, warm_done)
    b.emit("gpsimd", lambda e: e.dma_start(
        out=ar[:, WKV:WKV + 512], in_=wkv_d), inc=sWS, selfsync=False)
    b.emit("gpsimd", lambda e: e.dma_start(
        out=ar[:, WKU:WKU + 1024], in_=wku_d), inc=sWS, selfsync=False)
    b.emit("gpsimd", lambda e: e.dma_start(
        out=ar[:, WVU:WVU + 1024], in_=wvu_d), inc=sWS, selfsync=False)
    ws_all = sWS.n

    wq_base = [WQ0, WQ1]
    wo_base = [WO0, WO1]

    def load_wq(h, extra_waits=()):
        b.emit("gpsimd", lambda e, h=h: e.dma_start(
            out=ar[:, wq_base[h % 2]:wq_base[h % 2] + 2048], in_=wq_d[h]),
            waits=extra_waits, inc=sWQ[h % 2], selfsync=False)

    def load_wo(h, extra_waits=()):
        b.emit("gpsimd", lambda e, h=h: e.dma_start(
            out=ar[:, wo_base[h % 2]:wo_base[h % 2] + 2048], in_=wo_d[h]),
            waits=extra_waits, inc=sWO[h % 2], selfsync=False)

    load_wq(0)
    load_wo(0)
    load_wq(1, extra_waits=[(sX[2], 16)])
    load_wo(1)
    b.emit("gpsimd", lambda e: e.memset(ar[64:128, QP:QP + 4096], 0.0),
           waits=[(sPE, warm_done)], inc=sPOOL, selfsync=False)
    pool_qp = sPOOL.n

    xh_attn = []
    xh_ffn = []
    kv_ready_tb = [None] * NTB
    kvt_ready = [None]
    qpev = [[None] * NTB for _ in range(H)]
    prepA_ev = [None] * H
    wat_done = [None, None]
    msb_done = [None] * 4
    ut_done = [[None, None] for _ in range(2)]
    q_mm_done = [0] * H
    wo_mm_done = [0] * H
    wo_add_last = [0] * NCH
    pb_rot = [0]
    s_rot = [0]
    wo_pair = [0]
    exp_done = {}
    zev_done = {}
    emitted = set()
    pre_ln2 = [None]

    proj_set = [[("PB", 0), ("PB", 1)]]

    def proj_bank():
        banks = proj_set[0]
        k = banks[pb_rot[0] % len(banks)]
        pb_rot[0] += 1
        return k, bank({"PB": PB, "PC": PC}[k[0]], k[1])

    def emit_prepA(h):
        key = ("prepA", h)
        if key in emitted:
            return
        emitted.add(key)
        wqb = wq_base[h % 2]
        g = None
        for half in (0, 1):
            g = pe_group(
                PD[0:64, half * 512:(half + 1) * 512], [("PD", half)],
                [(ar[:, WKU + (h * 2 + c) * 64: WKU + (h * 2 + c + 1) * 64],
                  ar[:, wqb + c * 1024 + half * 512:
                     wqb + c * 1024 + (half + 1) * 512])
                 for c in (0, 1)],
                waits=[(sWS, ws_all), (sWQ[h % 2], 16 * (h // 2 + 1))])
        q_mm_done[h] = g
        prepA_ev[h] = consume("vector", lambda e: e.tensor_copy(
            ar[0:64, WAB:WAB + 1024], PD[0:64, 0:1024]),
            [("PD", 0), ("PD", 1)], waits=[(sPE, g)], inc=sDVE)

    def emit_prepB(h, on_pa=False):
        key = ("prepB", h)
        if key in emitted:
            return
        emitted.add(key)
        P, bk = (PA, ("PA", 0)) if on_pa else (PD, ("PD", 0))
        w = [(sDVE, prepA_ev[h]), (sPOOL, pool_init)]
        tf = bank_free.get(bk)
        if tf:
            w.append(tf)
        for i in range(8):
            b.emit("tensor", lambda e, i=i, P=P: e.matmul(
                P[:, i * 64:(i + 1) * 64],
                ar[0:64, WAB + i * 128: WAB + (i + 1) * 128],
                ident[0:64, 0:64], start=True, stop=True),
                waits=w if i == 0 else (),
                inc=sPE if i == 7 else None)
        g = sPE.n
        v = consume("vector", lambda e, P=P, h=h: e.tensor_copy(
            ar[:, WAT[h % 2]:WAT[h % 2] + 512], P[:, 0:512]),
            [bk], waits=[(sPE, g)], inc=sDVE)
        wat_done[h % 2] = (sDVE, v)

    def emit_prepC(h):
        key = ("prepC", h)
        if key in emitted:
            return
        emitted.add(key)
        wob = wo_base[h % 2]
        r0 = (h % 2) * 64
        g = None
        for half in (0, 1):
            g = pe_group(
                PD[r0:r0 + 64, half * 512:(half + 1) * 512], [("PD", half)],
                [(ar[:, WVU + (h * 2 + c) * 64: WVU + (h * 2 + c + 1) * 64],
                  ar[:, wob + c * 1024 + half * 512:
                     wob + c * 1024 + (half + 1) * 512])
                 for c in (0, 1)],
                waits=[(sWS, ws_all), (sWO[h % 2], 16 * (h // 2 + 1))])
        wo_mm_done[h] = g
        ms = MSB[(h // 2) % 2]
        v = consume("vector", lambda e, r0=r0, ms=ms: e.tensor_copy(
            ar[r0:r0 + 64, ms:ms + 1024], PD[r0:r0 + 64, 0:1024]),
            [("PD", 0), ("PD", 1)], waits=[(sPE, g)], inc=sDVE)
        msb_done[h % 4] = (sDVE, v)

    def emit_Qp(h, tb, eng="vector"):
        key = ("QP", h, tb)
        if key in emitted:
            return
        emitted.add(key)
        qk, qa = proj_bank()
        g = pe_group(
            qa[0:64, :], [qk],
            [(ar[:, WAT[h % 2] + c * 64: WAT[h % 2] + (c + 1) * 64],
              ar[:, XH + c * T + tb * 512: XH + c * T + (tb + 1) * 512])
             for c in range(NCH)],
            waits=[wat_done[h % 2], (sDVE, xh_attn[tb])])
        qpb = QP + (h % 2) * 2048
        dst = ar[0:64, qpb + tb * 512: qpb + (tb + 1) * 512]
        if eng == "vector":
            v = consume("vector", lambda e, a=qa, d=dst: e.tensor_copy(
                d, a[0:64, :]), [qk], waits=[(sPE, g)], inc=sDVE)
            qpev[h][tb] = (sDVE, v)
        else:
            v = consume("scalar", lambda e, a=qa, d=dst: e.activation(
                d, a[0:64, :], AF.Copy), [qk], waits=[(sPE, g)], inc=sACT)
            qpev[h][tb] = (sACT, v)

    def emit_kv(tb):
        g = pe_group(
            bank(PC, tb % 2)[0:64, :], [("PC", tb % 2)],
            [(ar[:, WKV + c * 64:WKV + (c + 1) * 64],
              ar[:, XH + c * T + tb * 512: XH + c * T + (tb + 1) * 512])
             for c in range(NCH)],
            waits=[(sWS, ws_all), (sDVE, xh_attn[tb])])
        kv_ready_tb[tb] = consume(
            "vector", lambda e, tb=tb: e.tensor_copy(
                ar[0:64, KVA + tb * 512:KVA + (tb + 1) * 512],
                bank(PC, tb % 2)[0:64, :]),
            [("PC", tb % 2)], waits=[(sPE, g)], inc=sDVE)

    def emit_kvt():
        key = ("kvt",)
        if key in emitted:
            return
        emitted.add(key)
        for oct_ in (0, 1):
            P = (PA, PD)[oct_]
            bk = ("PA", 0) if oct_ == 0 else ("PD", 0)
            w = [(sDVE, kv_ready_tb[NTB - 1]), (sPOOL, pool_init)]
            tf = bank_free.get(bk)
            if tf:
                w.append(tf)
            for i in range(8):
                kc = oct_ * 8 + i
                b.emit("tensor", lambda e, kc=kc, i=i, P=P: e.matmul(
                    P[:, i * 64:(i + 1) * 64],
                    ar[0:64, KVA + kc * 128: KVA + (kc + 1) * 128],
                    ident[0:64, 0:64], start=True, stop=True),
                    waits=w if i == 0 else (),
                    inc=sPE if i == 7 else None)
            g = sPE.n
            dst = (ar[:, KVT + oct_ * 8 * 65: KVT + (oct_ * 8 + 8) * 65]
                   .rearrange("p (t c) -> p t c", c=65)[:, :, 0:64])
            sc = P[:, 0:512].rearrange("p (t c) -> p t c", c=64)
            consume("vector", lambda e, d=dst, s=sc: e.tensor_copy(d, s),
                    [bk], waits=[(sPE, g)], inc=sDVE)
        kvt_ready[0] = (sDVE, sDVE.n)

    def emit_Spair(h, j, kp):
        key = ("SP", h, j, kp)
        if key in emitted:
            return
        emitted.add(key)
        bi = s_rot[0] % 2
        s_rot[0] += 1
        P = (PA, PD)[bi]
        keys = [("PA", 0), ("PA", 1)] if bi == 0 else [("PD", 0), ("PD", 1)]
        qpb = QP + (h % 2) * 2048
        g = None
        for hf in (0, 1):
            kc = 2 * kp + hf
            g = pe_group(
                bank(P, hf), [keys[hf]],
                [(ar[:, KVA + kc * 128: KVA + (kc + 1) * 128],
                  ar[:, qpb + j * 512: qpb + (j + 1) * 512])],
                waits=[qpev[h][j], (sDVE, kv_ready_tb[kp // 2]),
                       (sPOOL, pool_qp)])
        pbuf = RA + (j % 2) * 8192
        ew = [(sPE, g)]
        if h == 0 and j <= 1:
            ew.append((sDVE, xh_attn[1 if kp % 2 == 0 else NTB - 1]))
        v = b.emit(
            "scalar", lambda e, P=P, p=pbuf, kp=kp: e.activation(
                ar[:, p + kp * 1024: p + (kp + 1) * 1024], P[:, 0:1024],
                AF.Exp, scale=SC_ATTN),
            waits=ew, inc=sACT)
        exp_done[(h, j)] = v
        bank_free[keys[0]] = (sACT, v)
        bank_free[keys[1]] = (sACT, v)

    def emit_PVg(h, j, qs):
        pbuf = RA + (j % 2) * 8192
        zk = ("PC", qs % 2)
        za = bank(PC, qs % 2)
        g = pe_group(
            za[:, 0:65], [zk],
            [(ar[:, pbuf + kc * 512 + qs * 128: pbuf + kc * 512 + (qs + 1) * 128],
              ar[:, KVT + kc * 65: KVT + (kc + 1) * 65])
             for kc in range(NKC)],
            waits=[(sACT, exp_done[(h, j)]), kvt_ready[0]])
        b.emit("vector", lambda e, a=za: e.tensor_copy(t1[:, 0:1], a[:, 64:65]),
               waits=[(sPE, g)], inc=sDVE)
        b.emit("vector", lambda e: e.reciprocal(rc, t1[:, 0:1]), inc=sDVE)
        tc = j * 4 + qs
        zev_done[(h, j, qs)] = consume(
            "vector", lambda e, a=za, tc=tc: e.tensor_scalar_mul(
                ar[:, UTM + tc * 64: UTM + (tc + 1) * 64], a[:, 0:64], rc),
            [zk], waits=(), inc=sDVE)

    def emit_UT(h, oct_):
        key = ("UT", h, oct_)
        if key in emitted:
            return
        emitted.add(key)
        P = PB
        keys = [("PB", 0), ("PB", 1)]
        w = [(sDVE, zev_done[(h, 2 * oct_ + 1, 3)]), (sPOOL, pool_init)]
        for k in keys:
            tf = bank_free.get(k)
            if tf:
                w.append(tf)
        r0 = (h % 2) * 64
        for i in range(8):
            tc = oct_ * 8 + i
            b.emit("tensor", lambda e, tc=tc, i=i, P=P, r0=r0: e.matmul(
                P[r0:r0 + 64, i * 128:(i + 1) * 128],
                ar[:, UTM + tc * 64: UTM + (tc + 1) * 64],
                ident, start=True, stop=True),
                waits=w if i == 0 else (),
                inc=sPE if i == 7 else None)
        g = sPE.n
        ub = UFM[(h // 2) % 2]
        dst = ar[r0:r0 + 64, ub + oct_ * 1024: ub + (oct_ + 1) * 1024]
        v = consume("scalar", lambda e, P=P, d=dst, r0=r0: e.activation(
            d, P[r0:r0 + 64, 0:1024], AF.Copy),
            keys, waits=[(sPE, g)], inc=sACT)
        ut_done[h % 2][oct_] = (sACT, v)

    def emit_WO2(h1, o, oct_):
        P, pk = ((PB, "PB"), (PC, "PC"))[wo_pair[0] % 2]
        wo_pair[0] += 1
        h0 = h1 - 1
        ps = (h1 // 2) % 2
        g = None
        for n in (0, 1):
            tb = oct_ * 2 + n
            g = pe_group(
                bank(P, n), [(pk, n)],
                [(ar[:, MSB[ps] + o * 128: MSB[ps] + (o + 1) * 128],
                  ar[:, UFM[ps] + tb * 512: UFM[ps] + (tb + 1) * 512])],
                waits=[msb_done[h0 % 4], msb_done[h1 % 4],
                       ut_done[0][oct_], ut_done[1][oct_]])
        wo_add_last[o] = consume(
            "vector", lambda e, P=P, o=o, oc=oct_: e.tensor_add(
                x[:, o * T + oc * 1024: o * T + (oc + 1) * 1024], P[:, 0:1024],
                x[:, o * T + oc * 1024: o * T + (oc + 1) * 1024]),
            [(pk, 0), (pk, 1)],
            waits=[(sPE, g)] + [(s, 16) for s in sX], inc=sDVE)

    def emit_ln(first, out_ready, pre=None, after_tb=None, pre_tb=None):
        xb_tb = [None] * NTB
        xsq_tb = [None] * NTB

        def emit_pre(tb):
            if pre is not None:
                xb_tb[tb] = list(pre[0][tb])
                xsq_tb[tb] = list(pre[1][tb])
                return
            x3 = (x.rearrange("p (c n) -> p c n", c=NCH)
                  [:, :, tb * 512:(tb + 1) * 512])
            xb3 = (ar[:, RA:RA + NCH * T].rearrange("p (c n) -> p c n", c=NCH)
                   [:, :, tb * 512:(tb + 1) * 512])
            xq3 = (ar[:, XH:XH + NCH * T].rearrange("p (c n) -> p c n", c=NCH)
                   [:, :, tb * 512:(tb + 1) * 512])
            cv = b.emit("scalar", lambda e, o=xb3, i=x3: e.activation(
                o, i, AF.Copy), waits=[(sX[tb], 16)], inc=sACT)
            sv = b.emit("scalar", lambda e, o=xq3, i=x3: e.activation(
                o, i, AF.Square), inc=sACT)
            xb_tb[tb] = [(sACT, cv)]
            xsq_tb[tb] = [(sACT, sv)]

        emit_pre(0)
        emit_pre(1)

        srow = [(PA[0:1, 0:512], PA[0:1, 512:1024], ("PA", 0), ("PA", 1)),
                (PD[0:1, 0:512], PD[0:1, 512:1024], ("PD", 0), ("PD", 1))]
        stats_done = [None] * NTB

        def emit_stats(tb):
            r0, r1, k0, k1 = srow[tb % 2]
            sum_done = pe_group(
                r0, [k0],
                [(ones_bf, ar[:, RA + c * T + tb * 512: RA + c * T + (tb + 1) * 512])
                 for c in range(NCH)],
                waits=list(xb_tb[tb]) + [(sPOOL, pool_init)])
            sumsq_done = pe_group(
                r1, [k1],
                [(ones_bf, ar[:, XH + c * T + tb * 512: XH + c * T + (tb + 1) * 512])
                 for c in range(NCH)],
                waits=list(xsq_tb[tb]))
            stats_done[tb] = (sum_done, sumsq_done)

        r0a, r1a, k0a, k1a = srow[0]
        r0b, r1b, k0b, k1b = srow[1]
        sum0 = pe_group(r0a, [k0a],
                        [(ones_bf, ar[:, RA + c * T: RA + c * T + 512])
                         for c in range(NCH)],
                        waits=list(xb_tb[0]) + [(sPOOL, pool_init)])
        sq0 = pe_group(r1a, [k1a],
                       [(ones_bf, ar[:, XH + c * T: XH + c * T + 512])
                        for c in range(NCH)],
                       waits=list(xsq_tb[0]))
        sum1 = pe_group(r0b, [k0b],
                        [(ones_bf, ar[:, RA + c * T + 512: RA + c * T + 1024])
                         for c in range(NCH)],
                        waits=list(xb_tb[1]))
        sq1 = pe_group(r1b, [k1b],
                       [(ones_bf, ar[:, XH + c * T + 512: XH + c * T + 1024])
                        for c in range(NCH)],
                       waits=list(xsq_tb[1]))
        stats_done[0] = (sum0, sq0)
        stats_done[1] = (sum1, sq1)

        for tb in range(NTB):
            r0, r1, k0, k1 = srow[tb % 2]
            ro = (tb % 2) * 1536
            bo = (tb % 2) * 512
            mbcs = mbc[:, bo:bo + 512]
            rbcs = rbc[:, bo:bo + 512]
            sum_done, sumsq_done = stats_done[tb]
            v_mean = b.emit(
                "vector", lambda e, r0=r0, ro=ro: e.tensor_scalar_mul(
                    rows[0:1, ro:ro + 512], r0, 1.0 / 1024.0),
                waits=[(sPE, sum_done)], inc=sDVE)
            b.emit("vector", lambda e, r0=r0, ro=ro: e.tensor_mul(
                rows[0:1, ro + 1024:ro + 1536], rows[0:1, ro:ro + 512], r0),
                inc=sDVE)
            v_var = b.emit("vector", lambda e, r1=r1, ro=ro: e.tensor_sub(
                rows[0:1, ro + 1024:ro + 1536], r1,
                rows[0:1, ro + 1024:ro + 1536]),
                waits=[(sPE, sumsq_done)], inc=sDVE)
            bank_free[k0] = (sDVE, v_var)
            bank_free[k1] = (sDVE, v_var)
            if pre_tb is not None:
                pre_tb(tb)
            v_vr = b.emit("vector", lambda e, ro=ro: e.tensor_scalar(
                rows[0:1, ro + 512:ro + 1024], rows[0:1, ro + 1024:ro + 1536],
                1.0 / 1024.0, EPS, ALU.mult, ALU.add), inc=sDVE)
            a_ln = b.emit("scalar", lambda e, ro=ro: e.activation(
                rows[0:1, ro + 1024:ro + 1536], rows[0:1, ro + 512:ro + 1024],
                AF.Ln), waits=[(sDVE, v_vr)], inc=sACT)
            a_rstd = b.emit("scalar", lambda e, ro=ro: e.activation(
                rows[0:1, ro + 512:ro + 1024], rows[0:1, ro + 1024:ro + 1536],
                AF.Exp, scale=-0.5), waits=[(sACT, a_ln)], inc=sACT)
            vm_b = b.emit("vector", lambda e, ro=ro, m=mbcs: e.tensor_copy(
                m[0:1, :], rows[0:1, ro:ro + 512]),
                waits=[(sDVE, v_mean)], inc=sDVE)
            vr_b = b.emit("vector", lambda e, ro=ro, r=rbcs: e.tensor_copy(
                r[0:1, :], rows[0:1, ro + 512:ro + 1024]),
                waits=[(sACT, a_rstd)], inc=sDVE)
            bc_m = pe_group(PB[:, 0:512], [("PB", 0)],
                            [(onesb_row, mbcs[0:1, :])],
                            waits=[(sDVE, vm_b), (sPOOL, pool_init)])
            bc_r = pe_group(PB[:, 512:1024], [("PB", 1)],
                            [(onesb_row, rbcs[0:1, :])],
                            waits=[(sDVE, vr_b)])
            if first:
                cm = consume("vector", lambda e, m=mbcs: e.tensor_copy(
                    m, PB[:, 0:512]), [("PB", 0)],
                    waits=[(sPE, bc_m)], inc=sDVE)
                cr = consume("vector", lambda e, r=rbcs: e.tensor_copy(
                    r, PB[:, 512:1024]), [("PB", 1)],
                    waits=[(sPE, bc_r)], inc=sDVE)
                crw = (sDVE, cr)
            else:
                cm = consume("scalar", lambda e, m=mbcs: e.activation(
                    m, PB[:, 0:512], AF.Copy), [("PB", 0)],
                    waits=[(sPE, bc_m)], inc=sACT)
                cr = consume("scalar", lambda e, r=rbcs: e.activation(
                    r, PB[:, 512:1024], AF.Copy), [("PB", 1)],
                    waits=[(sPE, bc_r)], inc=sACT)
                crw = (sACT, cr)
            xh3 = (ar[:, XH:XH + NCH * T].rearrange("p (c n) -> p c n", c=NCH)
                   [:, :, tb * 512:(tb + 1) * 512])
            xb3 = (ar[:, RA:RA + NCH * T].rearrange("p (c n) -> p c n", c=NCH)
                   [:, :, tb * 512:(tb + 1) * 512])
            mb3 = mbcs.unsqueeze(1).broadcast_to([128, NCH, 512])
            rb3 = rbcs.unsqueeze(1).broadcast_to([128, NCH, 512])
            b.emit("vector", lambda e, o=xh3, a=xb3, m=mb3: e.tensor_sub(o, a, m),
                   waits=[crw], inc=sDVE)
            vv = b.emit("vector", lambda e, o=xh3, r=rb3: e.tensor_mul(o, o, r),
                        inc=sDVE)
            out_ready.append(vv)
            if after_tb is not None:
                after_tb(tb)
            if tb + 2 < NTB:
                emit_pre(tb + 2)
                emit_stats(tb + 2)
        return out_ready

    def ln1_pre(tb):
        if tb == 1:
            emit_kv(0)
            emit_prepA(0)
        elif tb == 2:
            emit_kv(1)
            emit_prepB(0, on_pa=True)
            emit_Qp(0, 0)
        elif tb == 3:
            emit_Spair(0, 0, 0)
            emit_Spair(0, 0, 2)
            emit_kv(2)
            emit_prepC(0)

    def ln1_after(tb):
        if tb == 3:
            emit_kv(3)
            emit_Spair(0, 0, 4)
            emit_kvt()

    emit_ln(True, xh_attn, after_tb=ln1_after, pre_tb=ln1_pre)
    maybe_tap("xh1", ar[:, XH:XH + NCH * T], [(sDVE, xh_attn[-1])])
    maybe_tap("kv", ar[0:64, KVA:KVA + T], [(sDVE, kv_ready_tb[-1])])
    maybe_tap("wab0", ar[0:64, WAB:WAB + 1024], [(sDVE, prepA_ev[0])])

    for h in range(H):
        nxt = h + 1
        emit_prepA(h)
        emit_prepB(h)
        emit_prepC(h)
        if h > 0:
            for tb in range(NTB):
                emit_Qp(h, tb)
        for kp in range(8):
            emit_Spair(h, 0, kp)
            if h == 0 and kp < 3:
                emit_Qp(0, kp + 1)
        if h == 0:
            maybe_tap("qp0", ar[0:64, QP:QP + 2048], [qpev[0][NTB - 1]])
            maybe_tap("p0", ar[:, RA:RA + 8192], [(sACT, exp_done[(0, 0)])])
        fill = {}
        if nxt < H:
            fill = {(1, 1): [lambda: emit_prepA(nxt)],
                    (2, 1): [lambda: emit_prepB(nxt)],
                    (2, 3): [lambda: emit_Qp(nxt, 0)],
                    (3, 0): [lambda: emit_Qp(nxt, 1)],
                    (3, 1): [lambda: emit_prepC(nxt)],
                    (3, 2): [lambda: emit_Qp(nxt, 2)],
                    (3, 3): [lambda: emit_Qp(nxt, 3)]}
        wu = []
        if h % 2 == 0 and h > 0:
            wu = [(h - 1, o, oc) for o in range(4) for oc in (0, 1)]
        elif h % 2 == 1 and h >= 3:
            wu = [(h - 2, o, oc) for o in range(4, NCH) for oc in (0, 1)]
        for idx, (h1, o, oc) in enumerate(wu):
            slot = (1 + idx // 4, idx % 4)
            fill.setdefault(slot, []).insert(0, (lambda h1=h1, o=o, oc=oc:
                                                 emit_WO2(h1, o, oc)))
        for j in (1, 2, 3):
            for i in range(4):
                emit_Spair(h, j, 2 * i)
                emit_Spair(h, j, 2 * i + 1)
                emit_PVg(h, j - 1, i)
                for f in fill.get((j, i), ()):
                    f()
        if h + 2 < H:
            load_wq(h + 2, extra_waits=[(sPE, q_mm_done[h])])
            load_wo(h + 2, extra_waits=[(sPE, wo_mm_done[h])])

        emit_PVg(h, 3, 0)
        if nxt < H:
            emit_Spair(nxt, 0, 0)
        emit_PVg(h, 3, 1)
        if nxt < H:
            emit_Spair(nxt, 0, 1)
        emit_PVg(h, 3, 2)
        if nxt < H:
            emit_Spair(nxt, 0, 2)
        emit_PVg(h, 3, 3)
        if nxt < H:
            emit_Spair(nxt, 0, 3)
        emit_UT(h, 0)
        if nxt < H:
            emit_Spair(nxt, 0, 4)
        emit_UT(h, 1)
        if nxt < H:
            emit_Spair(nxt, 0, 5)
            emit_Spair(nxt, 0, 6)
            emit_Spair(nxt, 0, 7)
            if h % 2 == 1:
                emit_Spair(nxt, 1, 0)
                emit_Spair(nxt, 1, 1)
                emit_Spair(nxt, 1, 2)
                emit_Spair(nxt, 1, 3)
        else:
            xb_l = [None] * NTB
            xsq_l = [None] * NTB
            for oct_ in (0, 1):
                for o in range(NCH):
                    emit_WO2(h, o, oct_)
                av = wo_add_last[NCH - 1]
                sh = (sXB, sXB2)[oct_]
                x3h = (x.rearrange("p (c n) -> p c n", c=NCH)
                       [:, :, oct_ * 1024:(oct_ + 1) * 1024])
                xb3h = (ar[:, RA:RA + NCH * T]
                        .rearrange("p (c n) -> p c n", c=NCH)
                        [:, :, oct_ * 1024:(oct_ + 1) * 1024])
                b.emit("gpsimd", lambda e, o=xb3h, i=x3h: e.dma_start(
                    out=o, in_=i),
                    waits=[(sDVE, av)], inc=sh, selfsync=False)
                for tb in (oct_ * 2, oct_ * 2 + 1):
                    x3 = (x.rearrange("p (c n) -> p c n", c=NCH)
                          [:, :, tb * 512:(tb + 1) * 512])
                    xq3 = (ar[:, XH:XH + NCH * T]
                           .rearrange("p (c n) -> p c n", c=NCH)
                           [:, :, tb * 512:(tb + 1) * 512])
                    sv = b.emit("scalar", lambda e, o=xq3, i=x3: e.activation(
                        o, i, AF.Square), waits=[(sDVE, av)], inc=sACT)
                    xb_l[tb] = [(sh, 16)]
                    xsq_l[tb] = [(sACT, sv)]
            pre_ln2[0] = (xb_l, xsq_l)
            wo_mm_done[H - 1] = sPE.n

        if h == 0:
            maybe_tap("u0", ar[:, UTM:UTM + 1024], [(sDVE, zev_done[(0, 3, 3)])])
            maybe_tap("uf0", ar[:, UFM[0]:UFM[0] + 2048], [ut_done[0][1]])

    pe_attn_done = sPE.n
    maybe_tap("xa", x[:, 0:NCH * T], [(sDVE, max(wo_add_last))])

    wg_load_done = [None] * NFC
    gate_done = [None] * NFC
    up_done = [None] * NFC
    hmul_done = [[None, None] for _ in range(NFC)]

    def load_wg(c, waits):
        b.emit("gpsimd", lambda e, c=c: e.dma_start(
            out=ar[:, WG[c % 2]:WG[c % 2] + 1024], in_=wg_d[c]),
            waits=waits, inc=sWG[c % 2], selfsync=False)

    def load_wu(c, waits):
        b.emit("gpsimd", lambda e, c=c: e.dma_start(
            out=ar[:, WU[c % 2]:WU[c % 2] + 1024], in_=wu_d[c]),
            waits=waits, inc=sWU[c % 2], selfsync=False)

    def load_wd(c, waits):
        ci = c % NHALF
        b.emit("gpsimd", lambda e, c=c, ci=ci: e.dma_start(
            out=ar[:, WDB + ci * 1024:WDB + (ci + 1) * 1024],
            in_=wd_d[c]),
            waits=waits, inc=sWD, selfsync=False)

    load_wg(0, [(sPE, q_mm_done[6])])
    load_wg(1, [])
    load_wu(0, [(sPE, q_mm_done[7])])
    load_wu(1, [])
    for c in range(NHALF):
        load_wd(c, [(sPE, wo_mm_done[7])] if c == 0 else [])

    G_ap = [PA, PC]
    U_ap = [PB, PD]
    G_keys = [[("PA", 0), ("PA", 1)], [("PC", 0), ("PC", 1)]]
    U_keys = [[("PB", 0), ("PB", 1)], [("PD", 0), ("PD", 1)]]

    scr_free = [None]
    down_b1_done = [0]

    def emit_ffn_A(c, t):
        key = ("A", c, t)
        if key in emitted:
            return
        emitted.add(key)
        half = c // NHALF
        ci = c % NHALF
        gg = None
        for n in range(2):
            tb = t * 2 + n
            gg = pe_group(
                bank(G_ap[t], n), [G_keys[t][n]],
                [(ar[:, WG[c % 2] + k * 128: WG[c % 2] + (k + 1) * 128],
                  ar[:, XH + k * T + tb * 512: XH + k * T + (tb + 1) * 512])
                 for k in range(NCH)],
                waits=[(sWG[c % 2], 16 * (c // 2 + 1)), (sDVE, xh_ffn[tb])])
        gate_done[c] = max(gate_done[c] or 0, gg)
        ug = None
        for n in range(2):
            tb = t * 2 + n
            ug = pe_group(
                bank(U_ap[t], n), [U_keys[t][n]],
                [(ar[:, WU[c % 2] + k * 128: WU[c % 2] + (k + 1) * 128],
                  ar[:, XH + k * T + tb * 512: XH + k * T + (tb + 1) * 512])
                 for k in range(NCH)],
                waits=[(sWU[c % 2], 16 * (c // 2 + 1)), (sDVE, xh_ffn[tb])])
        up_done[c] = max(up_done[c] or 0, ug)
        scr_w = [(sPE, wo_mm_done[6])] if scr_free[0] is None else [scr_free[0]]
        a_si = consume(
            "scalar", lambda e, t=t: e.activation(
                ar[:, SCR_S:SCR_S + 1024], G_ap[t][:, 0:1024], AF.Silu,
                scale=INV127),
            G_keys[t], waits=[(sPE, gg)] + scr_w, inc=sACT)
        a_up = consume(
            "scalar", lambda e, t=t: e.activation(
                ar[:, SCR_U:SCR_U + 1024], U_ap[t][:, 0:1024], AF.Copy,
                scale=INV127 * INV127),
            U_keys[t], waits=[(sPE, ug)], inc=sACT)
        hcol = HHB + ci * T + t * 1024
        hw = [(sACT, a_up)]
        if half == 1:
            hw.append((sPE, down_b1_done[0]))
        hmul_done[c][t] = b.emit(
            "vector", lambda e, hcol=hcol: e.tensor_mul(
                ar[:, hcol:hcol + 1024],
                ar[:, SCR_S:SCR_S + 1024], ar[:, SCR_U:SCR_U + 1024]),
            waits=hw, inc=sDVE)
        scr_free[0] = (sDVE, hmul_done[c][t])
        if t == 1 and c + 2 < NFC:
            load_wg(c + 2, [(sPE, gate_done[c])])
            load_wu(c + 2, [(sPE, up_done[c])])

    def ln2_pre(tb):
        if tb == 2:
            emit_ffn_A(0, 0)
        elif tb == 3:
            emit_ffn_A(1, 0)

    emit_ln(False, xh_ffn, pre=pre_ln2[0], pre_tb=ln2_pre)
    maybe_tap("xh2", ar[:, XH:XH + NCH * T], [(sDVE, xh_ffn[-1])])

    down_add_final = None
    for half in range(2):
        c0 = half * NHALF
        order = [(0, 0), (1, 0), (0, 1), (1, 1)]
        order += [(ci, t) for ci in range(2, NHALF) for t in range(2)]
        for (ci, t) in order:
            emit_ffn_A(c0 + ci, t)
        if half == 0:
            maybe_tap("h0", ar[:, HHB:HHB + NHALF * T],
                      [(sDVE, hmul_done[NHALF - 1][1])])

        d_rot = 0
        for o in range(NCH):
            for tp in range(2):
                P, pk = ((PA, "PA"), (PB, "PB"))[d_rot % 2]
                d_rot += 1
                g = None
                for n in (0, 1):
                    tb = tp * 2 + n
                    g = pe_group(
                        bank(P, n), [(pk, n)],
                        [(ar[:, WDB + ci * 1024 + o * 128: WDB + ci * 1024 + (o + 1) * 128],
                          ar[:, HHB + ci * T + tb * 512: HHB + ci * T + (tb + 1) * 512])
                         for ci in range(NHALF)],
                        waits=[(sWD, 16 * NHALF * (half + 1)),
                               (sDVE, hmul_done[c0 + NHALF - 1][tp])])
                down_add_final = consume(
                    "vector", lambda e, P=P, o=o, tp=tp: e.tensor_add(
                        x[:, o * T + tp * 1024: o * T + (tp + 1) * 1024], P[:, 0:1024],
                        x[:, o * T + tp * 1024: o * T + (tp + 1) * 1024]),
                    [(pk, 0), (pk, 1)], waits=[(sPE, g)], inc=sDVE)
            if half == 1:
                b.emit("sync", lambda e, o=o: e.dma_start(
                    out=out_d[o * 128:(o + 1) * 128, :],
                    in_=x[:, o * T:(o + 1) * T]),
                    waits=[(sDVE, down_add_final)], inc=sST)
        down_b1_done[0] = sPE.n
        if half == 0:
            for c in range(NHALF, NFC):
                load_wd(c, [(sPE, down_b1_done[0])] if c == NHALF else [])

    fin = sST.n
    b.emit("sync", lambda e: e.wait_ge(sST.h, fin))

    with nc.Block() as block:
        @block.tensor
        def _(eng):
            for f in b.q["tensor"]:
                f(eng)

        @block.scalar
        def _(eng):
            for f in b.q["scalar"]:
                f(eng)

        @block.vector
        def _(eng):
            for f in b.q["vector"]:
                f(eng)

        @block.gpsimd
        def _(eng):
            for f in b.q["gpsimd"]:
                f(eng)

        @block.sync
        def _(eng):
            for f in b.q["sync"]:
                f(eng)

    return nc


def _prep_inputs(L0_attn_x, L0_attn_Wq, L0_attn_Wkv_down, L0_attn_Wk_up,
                 L0_attn_Wv_up, L0_attn_Wo, L0_ffn_w_gate, L0_ffn_w_up,
                 L0_ffn_w_down):
    xs = np.asarray(L0_attn_x)
    Wq = np.asarray(L0_attn_Wq)
    Wkv = np.asarray(L0_attn_Wkv_down)
    Wku = np.asarray(L0_attn_Wk_up)
    Wvu = np.asarray(L0_attn_Wv_up)
    Wo = np.asarray(L0_attn_Wo)
    Wg = np.asarray(L0_ffn_w_gate)
    Wu = np.asarray(L0_ffn_w_up)
    Wd = np.asarray(L0_ffn_w_down)

    wq = np.ascontiguousarray(
        Wq.reshape(H, 2, 128, D).transpose(0, 2, 1, 3).reshape(H, 128, 2048))
    wkv = np.ascontiguousarray(
        Wkv.T.reshape(NCH, 128, LR).transpose(1, 0, 2).reshape(128, 512))
    wku = np.ascontiguousarray(
        Wku.reshape(H, 2, 128, LR).transpose(2, 0, 1, 3).reshape(128, 1024))
    wvu = np.ascontiguousarray(
        Wvu.reshape(H, 2, 128, LR).transpose(2, 0, 1, 3).reshape(128, 1024))
    wo = np.ascontiguousarray(
        Wo.T.reshape(H, 2, 128, D).transpose(0, 2, 1, 3).reshape(H, 128, 2048))
    wg = np.ascontiguousarray(
        Wg.T.reshape(NCH, 128, NFC, 128).transpose(2, 1, 0, 3).reshape(NFC, 128, 1024))
    wu = np.ascontiguousarray(
        Wu.T.reshape(NCH, 128, NFC, 128).transpose(2, 1, 0, 3).reshape(NFC, 128, 1024))
    wd = np.ascontiguousarray(Wd.T.reshape(NFC, 128, D))
    maps = []
    for i in range(xs.shape[0]):
        xq = np.ascontiguousarray(
            xs[i].T.reshape(NCH, 128, NTB, 512).transpose(2, 1, 0, 3))
        maps.append(dict(
            x=xq, wq=wq, wkv=wkv, wku=wku,
            wvu=wvu, wo=wo, wg=wg, wu=wu, wd=wd))
    return maps


def kernel(L0_attn_x, L0_attn_Wq, L0_attn_Wkv_down, L0_attn_Wk_up,
           L0_attn_Wv_up, L0_attn_Wo, L0_ffn_w_gate, L0_ffn_w_up,
           L0_ffn_w_down):
    maps = _prep_inputs(L0_attn_x, L0_attn_Wq, L0_attn_Wkv_down, L0_attn_Wk_up,
                        L0_attn_Wv_up, L0_attn_Wo, L0_ffn_w_gate, L0_ffn_w_up,
                        L0_ffn_w_down)
    nc = build_program()
    res = run_bass_kernel_spmd(nc, maps, core_ids=list(range(8)))
    out = np.stack([np.asarray(r["out"]).T for r in res.results])
    return np.ascontiguousarray(out.astype(np.float32))


# version 45
# speedup vs baseline: 1.0115x; 1.0072x over previous
import numpy as np

import concourse.bass as bass
import concourse.mybir as mybir
from concourse.bass_utils import run_bass_kernel_spmd

F32 = mybir.dt.float32
BF16 = mybir.dt.bfloat16
I32 = mybir.dt.int32
AF = mybir.ActivationFunctionType
ALU = mybir.AluOpType

T = 2048
D = 1024
H = 8
DH = 256
LR = 64
FH = 3072
NCH = 8
NTB = 4
NKC = 16
NFC = 24
NHALF = 12
INV127 = 1.0 / 127.0
SC_ATTN = 1.0 / 16.0
EPS = 1e-5

XH = 0
RA = 16384
QH = 32768
KH = 36864
VH = 40960
ZT = 45072
ZF = 49168
WQ0, WQ1 = 53264, 55312
WO0, WO1 = 57360, 59408
WKU = 43008
WVU = 44048
WKV = 50192
KVA = 50704
IDC = 52752
ONE = 53248
ARENA = 61456
QP = QH
KVT = KH
WAB = KH + 1040
WAT = (KH + 2064, KH + 2576)
MSB = (VH, VH + 1024)
UTM = 45072
UFM = (46096, 48144)
HHB = RA
WDB = VH
WG = (WQ0, WQ0 + 1024)
WU = (WQ1, WQ1 + 1024)
SCR_S = WO0
SCR_U = WO0 + 1024


class _Sem:
    __slots__ = ("h", "step", "n")


class _Builder:
    def __init__(self, nc):
        self.nc = nc
        self.q = {"tensor": [], "scalar": [], "vector": [], "gpsimd": [], "sync": []}
        self.waited = {k: {} for k in self.q}
        self.selfsem = {}

    def sem(self, name, step=1):
        s = _Sem()
        s.h = self.nc.alloc_semaphore(name)
        s.step = step
        s.n = 0
        return s

    def emit(self, eng, fn, waits=(), inc=None, selfsync=True):
        ss = self.selfsem.get(eng)
        if ss is not None and selfsync:
            waits = list(waits) + ([(ss, ss.n)] if ss.n > 0 else [])
            if inc is None:
                inc = ss
            else:
                assert inc is ss, f"{eng} compute ops must inc the engine sem"
        wl = []
        for (s, v) in waits:
            if s is None or v is None or v <= 0:
                continue
            assert v <= s.n, f"wait beyond emitted count on {eng}: {v} > {s.n}"
            w = self.waited[eng]
            k = id(s)
            if v > w.get(k, 0):
                w[k] = v
                wl.append((s.h, v))
        after = None
        if inc is not None:
            inc.n += inc.step
            after = (inc.h, inc.step)

        def run(e, wl=wl, fn=fn, after=after):
            for (hh, v) in wl:
                e.wait_ge(hh, v)
            inst = fn(e)
            if after is not None:
                inst.then_inc(after[0], after[1])

        self.q[eng].append(run)
        return inc.n if inc is not None else None


def build_program(tap=None):
    nc = bass.Bass("TRN2", target_bir_lowering=False)

    x_d = nc.dram_tensor("x", [NTB, 128, NCH, 512], F32, kind="ExternalInput").ap()
    wq_d = nc.dram_tensor("wq", [H, 128, 2048], F32, kind="ExternalInput").ap()
    wkv_d = nc.dram_tensor("wkv", [128, 512], F32, kind="ExternalInput").ap()
    wku_d = nc.dram_tensor("wku", [128, 1024], F32, kind="ExternalInput").ap()
    wvu_d = nc.dram_tensor("wvu", [128, 1024], F32, kind="ExternalInput").ap()
    wo_d = nc.dram_tensor("wo", [H, 128, 2048], F32, kind="ExternalInput").ap()
    wg_d = nc.dram_tensor("wg", [NFC, 128, 1024], I32, kind="ExternalInput").ap()
    wu_d = nc.dram_tensor("wu", [NFC, 128, 1024], I32, kind="ExternalInput").ap()
    wd_d = nc.dram_tensor("wd", [NFC, 128, D], I32, kind="ExternalInput").ap()
    out_d = nc.dram_tensor("out", [D, T], F32, kind="ExternalOutput").ap()

    x_t = nc.alloc_sbuf_tensor("x_fm", [128, NCH * T], F32)
    ar_t = nc.alloc_sbuf_tensor("arena", [128, ARENA], BF16)
    t1_t = nc.alloc_sbuf_tensor("t1", [128, 16], F32)
    rows_t = nc.alloc_sbuf_tensor("rows", [1, 3072], F32)
    rc_t = nc.alloc_sbuf_tensor("rc", [128, 1], F32)
    mbc_t = nc.alloc_sbuf_tensor("mbc", [128, 1024], BF16)
    rbc_t = nc.alloc_sbuf_tensor("rbc", [128, 1024], BF16)
    onesb_row_t = nc.alloc_sbuf_tensor("onesb_row", [1, 128], BF16)
    x = x_t.ap()
    ar = ar_t.ap()
    t1 = t1_t.ap()
    rows = rows_t.ap()
    rc = rc_t.ap()
    mbc = mbc_t.ap()
    rbc = rbc_t.ap()
    onesb_row = onesb_row_t.ap()

    PA = nc.alloc_psum_tensor("PA", [128, 1024], F32).ap()
    PB = nc.alloc_psum_tensor("PB", [128, 1024], F32).ap()
    PC = nc.alloc_psum_tensor("PC", [128, 1024], F32).ap()
    PD = nc.alloc_psum_tensor("PD", [128, 1024], F32).ap()

    def bank(P, i):
        return P[:, i * 512:(i + 1) * 512]

    b = _Builder(nc)
    sX = [b.sem(f"sX{i}", 16) for i in range(4)]
    sWS = b.sem("sWS", 16)
    sXB = b.sem("sXB", 16)
    sXB2 = b.sem("sXB2", 16)
    sWQ = [b.sem("sWQ0", 16), b.sem("sWQ1", 16)]
    sWO = [b.sem("sWO0", 16), b.sem("sWO1", 16)]
    sWG = [b.sem("sWG0", 16), b.sem("sWG1", 16)]
    sWU = [b.sem("sWU0", 16), b.sem("sWU1", 16)]
    sWD = b.sem("sWD", 16)
    sST = b.sem("sST", 16)
    sPE = b.sem("sPE", 1)
    sACT = b.sem("sACT", 1)
    sDVE = b.sem("sDVE", 1)
    sPOOL = b.sem("sPOOL", 1)
    b.selfsem = {"vector": sDVE, "scalar": sACT, "gpsimd": sPOOL}

    bank_free = {}

    def pe_group(out_ap, keys, pieces, waits=()):
        w = list(waits)
        for k in keys:
            bf = bank_free.get(k)
            if bf:
                w.append(bf)
        n = len(pieces)
        for i, (l, r) in enumerate(pieces):
            b.emit(
                "tensor",
                (lambda e, o=out_ap, l=l, r=r, st=(i == 0), sp=(i == n - 1):
                 e.matmul(o, l, r, start=st, stop=sp)),
                waits=w if i == 0 else (),
                inc=sPE if i == n - 1 else None,
            )
        return sPE.n

    def consume(eng, fn, keys, waits, inc):
        v = b.emit(eng, fn, waits=waits, inc=inc)
        for k in keys:
            bank_free[k] = (inc, v)
        return v

    def maybe_tap(name, src_ap, waits):
        if tap == name:
            td = nc.dram_tensor("tap", list(src_ap.shape), src_ap.dtype,
                                kind="ExternalOutput").ap()
            b.emit("sync", lambda e, s=src_ap, t=td: e.dma_start(out=t, in_=s),
                   waits=waits, inc=sST)

    wz = b.emit("gpsimd", lambda e: e.memset(ar[:, QH:QH + 512], 0.0), inc=sPOOL)
    ident = ar[:, IDC:IDC + 128]
    ones_bf = ar[:, ONE:ONE + 1]
    b.emit("gpsimd", lambda e: e.memset(ident, 0.0), inc=sPOOL)
    b.emit("gpsimd", lambda e: e.affine_select(
        out=ident, in_=ident,
        compare_op=ALU.not_equal, fill=1.0, base=0,
        pattern=[[-1, 128]], channel_multiplier=1), inc=sPOOL)
    b.emit("gpsimd", lambda e: e.memset(ones_bf, 1.0), inc=sPOOL)
    b.emit("gpsimd", lambda e: e.memset(onesb_row, 1.0), inc=sPOOL)
    kvt1 = ar[:, KVT:KVT + NKC * 65].rearrange("p (t c) -> p t c", c=65)[:, :, 64:65]
    b.emit("gpsimd", lambda e: e.memset(kvt1, 1.0), inc=sPOOL)
    b.emit("gpsimd", lambda e: e.memset(ar[64:128, KVA:KVA + 2048], 0.0), inc=sPOOL)
    pool_init = sPOOL.n
    for tb in range(NTB):
        b.emit("sync", lambda e, tb=tb: e.dma_start(
            out=x.rearrange("p (c n) -> p c n", c=NCH)
                [:, :, tb * 512:(tb + 1) * 512],
            in_=x_d[tb]),
            inc=sX[tb])
    for i in range(63):
        b.emit("tensor", lambda e: e.matmul(
            PD[:, 0:512], ar[:, QH:QH + 128], ar[:, QH:QH + 512],
            start=True, stop=True),
            waits=[(sPOOL, wz)] if i == 0 else (), inc=None)
    b.emit("tensor", lambda e: e.matmul(
        PD[:, 0:512], ar[:, QH:QH + 128], ar[:, QH:QH + 512],
        start=True, stop=True), inc=sPE)
    warm_done = sPE.n
    bank_free[("PD", 0)] = (sPE, warm_done)
    b.emit("gpsimd", lambda e: e.dma_start(
        out=ar[:, WKV:WKV + 512], in_=wkv_d), inc=sWS, selfsync=False)
    b.emit("gpsimd", lambda e: e.dma_start(
        out=ar[:, WKU:WKU + 1024], in_=wku_d), inc=sWS, selfsync=False)
    b.emit("gpsimd", lambda e: e.dma_start(
        out=ar[:, WVU:WVU + 1024], in_=wvu_d), inc=sWS, selfsync=False)
    ws_all = sWS.n

    wq_base = [WQ0, WQ1]
    wo_base = [WO0, WO1]

    def load_wq(h, extra_waits=()):
        b.emit("gpsimd", lambda e, h=h: e.dma_start(
            out=ar[:, wq_base[h % 2]:wq_base[h % 2] + 2048], in_=wq_d[h]),
            waits=extra_waits, inc=sWQ[h % 2], selfsync=False)

    def load_wo(h, extra_waits=()):
        b.emit("gpsimd", lambda e, h=h: e.dma_start(
            out=ar[:, wo_base[h % 2]:wo_base[h % 2] + 2048], in_=wo_d[h]),
            waits=extra_waits, inc=sWO[h % 2], selfsync=False)

    load_wq(0)
    load_wo(0)
    load_wq(1, extra_waits=[(sX[2], 16)])
    load_wo(1)
    b.emit("gpsimd", lambda e: e.memset(ar[64:128, QP:QP + 4096], 0.0),
           waits=[(sPE, warm_done)], inc=sPOOL, selfsync=False)
    pool_qp = sPOOL.n

    xh_attn = []
    xh_ffn = []
    kv_ready_tb = [None] * NTB
    kvt_ready = [None]
    qpev = [[None] * NTB for _ in range(H)]
    prepA_ev = [None] * H
    wat_done = [None, None]
    msb_done = [None] * 4
    ut_done = [[None, None] for _ in range(2)]
    q_mm_done = [0] * H
    wo_mm_done = [0] * H
    wo_add_last = [0] * NCH
    pb_rot = [0]
    s_rot = [0]
    wo_pair = [0]
    exp_done = {}
    zev_done = {}
    emitted = set()
    pre_ln2 = [None]

    proj_set = [[("PB", 0), ("PB", 1)]]

    def proj_bank():
        banks = proj_set[0]
        k = banks[pb_rot[0] % len(banks)]
        pb_rot[0] += 1
        return k, bank({"PB": PB, "PC": PC}[k[0]], k[1])

    def emit_prepA(h):
        key = ("prepA", h)
        if key in emitted:
            return
        emitted.add(key)
        wqb = wq_base[h % 2]
        g = None
        for half in (0, 1):
            g = pe_group(
                PD[0:64, half * 512:(half + 1) * 512], [("PD", half)],
                [(ar[:, WKU + (h * 2 + c) * 64: WKU + (h * 2 + c + 1) * 64],
                  ar[:, wqb + c * 1024 + half * 512:
                     wqb + c * 1024 + (half + 1) * 512])
                 for c in (0, 1)],
                waits=[(sWS, ws_all), (sWQ[h % 2], 16 * (h // 2 + 1))])
        q_mm_done[h] = g
        prepA_ev[h] = consume("vector", lambda e: e.tensor_copy(
            ar[0:64, WAB:WAB + 1024], PD[0:64, 0:1024]),
            [("PD", 0), ("PD", 1)], waits=[(sPE, g)], inc=sDVE)

    def emit_prepB(h, on_pa=False):
        key = ("prepB", h)
        if key in emitted:
            return
        emitted.add(key)
        P, bk = (PA, ("PA", 0)) if on_pa else (PD, ("PD", 0))
        w = [(sDVE, prepA_ev[h]), (sPOOL, pool_init)]
        tf = bank_free.get(bk)
        if tf:
            w.append(tf)
        for i in range(8):
            b.emit("tensor", lambda e, i=i, P=P: e.matmul(
                P[:, i * 64:(i + 1) * 64],
                ar[0:64, WAB + i * 128: WAB + (i + 1) * 128],
                ident[0:64, 0:64], start=True, stop=True),
                waits=w if i == 0 else (),
                inc=sPE if i == 7 else None)
        g = sPE.n
        v = consume("vector", lambda e, P=P, h=h: e.tensor_copy(
            ar[:, WAT[h % 2]:WAT[h % 2] + 512], P[:, 0:512]),
            [bk], waits=[(sPE, g)], inc=sDVE)
        wat_done[h % 2] = (sDVE, v)

    def emit_prepC(h):
        key = ("prepC", h)
        if key in emitted:
            return
        emitted.add(key)
        wob = wo_base[h % 2]
        r0 = (h % 2) * 64
        g = None
        for half in (0, 1):
            g = pe_group(
                PD[r0:r0 + 64, half * 512:(half + 1) * 512], [("PD", half)],
                [(ar[:, WVU + (h * 2 + c) * 64: WVU + (h * 2 + c + 1) * 64],
                  ar[:, wob + c * 1024 + half * 512:
                     wob + c * 1024 + (half + 1) * 512])
                 for c in (0, 1)],
                waits=[(sWS, ws_all), (sWO[h % 2], 16 * (h // 2 + 1))])
        wo_mm_done[h] = g
        ms = MSB[(h // 2) % 2]
        v = consume("vector", lambda e, r0=r0, ms=ms: e.tensor_copy(
            ar[r0:r0 + 64, ms:ms + 1024], PD[r0:r0 + 64, 0:1024]),
            [("PD", 0), ("PD", 1)], waits=[(sPE, g)], inc=sDVE)
        msb_done[h % 4] = (sDVE, v)

    def emit_Qp(h, tb, eng="vector"):
        key = ("QP", h, tb)
        if key in emitted:
            return
        emitted.add(key)
        qk, qa = proj_bank()
        g = pe_group(
            qa[0:64, :], [qk],
            [(ar[:, WAT[h % 2] + c * 64: WAT[h % 2] + (c + 1) * 64],
              ar[:, XH + c * T + tb * 512: XH + c * T + (tb + 1) * 512])
             for c in range(NCH)],
            waits=[wat_done[h % 2], (sDVE, xh_attn[tb])])
        qpb = QP + (h % 2) * 2048
        dst = ar[0:64, qpb + tb * 512: qpb + (tb + 1) * 512]
        if eng == "vector":
            v = consume("vector", lambda e, a=qa, d=dst: e.tensor_copy(
                d, a[0:64, :]), [qk], waits=[(sPE, g)], inc=sDVE)
            qpev[h][tb] = (sDVE, v)
        else:
            v = consume("scalar", lambda e, a=qa, d=dst: e.activation(
                d, a[0:64, :], AF.Copy), [qk], waits=[(sPE, g)], inc=sACT)
            qpev[h][tb] = (sACT, v)

    def emit_kv(tb):
        g = pe_group(
            bank(PC, tb % 2)[0:64, :], [("PC", tb % 2)],
            [(ar[:, WKV + c * 64:WKV + (c + 1) * 64],
              ar[:, XH + c * T + tb * 512: XH + c * T + (tb + 1) * 512])
             for c in range(NCH)],
            waits=[(sWS, ws_all), (sDVE, xh_attn[tb])])
        kv_ready_tb[tb] = consume(
            "vector", lambda e, tb=tb: e.tensor_copy(
                ar[0:64, KVA + tb * 512:KVA + (tb + 1) * 512],
                bank(PC, tb % 2)[0:64, :]),
            [("PC", tb % 2)], waits=[(sPE, g)], inc=sDVE)

    def emit_kvt():
        key = ("kvt",)
        if key in emitted:
            return
        emitted.add(key)
        for oct_ in (0, 1):
            P = (PA, PD)[oct_]
            bk = ("PA", 0) if oct_ == 0 else ("PD", 0)
            w = [(sDVE, kv_ready_tb[NTB - 1]), (sPOOL, pool_init)]
            tf = bank_free.get(bk)
            if tf:
                w.append(tf)
            for i in range(8):
                kc = oct_ * 8 + i
                b.emit("tensor", lambda e, kc=kc, i=i, P=P: e.matmul(
                    P[:, i * 64:(i + 1) * 64],
                    ar[0:64, KVA + kc * 128: KVA + (kc + 1) * 128],
                    ident[0:64, 0:64], start=True, stop=True),
                    waits=w if i == 0 else (),
                    inc=sPE if i == 7 else None)
            g = sPE.n
            dst = (ar[:, KVT + oct_ * 8 * 65: KVT + (oct_ * 8 + 8) * 65]
                   .rearrange("p (t c) -> p t c", c=65)[:, :, 0:64])
            sc = P[:, 0:512].rearrange("p (t c) -> p t c", c=64)
            consume("vector", lambda e, d=dst, s=sc: e.tensor_copy(d, s),
                    [bk], waits=[(sPE, g)], inc=sDVE)
        kvt_ready[0] = (sDVE, sDVE.n)

    def emit_Spair(h, j, kp):
        key = ("SP", h, j, kp)
        if key in emitted:
            return
        emitted.add(key)
        bi = s_rot[0] % 2
        s_rot[0] += 1
        P = (PA, PD)[bi]
        keys = [("PA", 0), ("PA", 1)] if bi == 0 else [("PD", 0), ("PD", 1)]
        qpb = QP + (h % 2) * 2048
        g = None
        for hf in (0, 1):
            kc = 2 * kp + hf
            g = pe_group(
                bank(P, hf), [keys[hf]],
                [(ar[:, KVA + kc * 128: KVA + (kc + 1) * 128],
                  ar[:, qpb + j * 512: qpb + (j + 1) * 512])],
                waits=[qpev[h][j], (sDVE, kv_ready_tb[kp // 2]),
                       (sPOOL, pool_qp)])
        pbuf = RA + (j % 2) * 8192
        ew = [(sPE, g)]
        if h == 0 and j <= 1:
            ew.append((sDVE, xh_attn[1 if kp % 2 == 0 else NTB - 1]))
        v = b.emit(
            "scalar", lambda e, P=P, p=pbuf, kp=kp: e.activation(
                ar[:, p + kp * 1024: p + (kp + 1) * 1024], P[:, 0:1024],
                AF.Exp, scale=SC_ATTN),
            waits=ew, inc=sACT)
        exp_done[(h, j)] = v
        bank_free[keys[0]] = (sACT, v)
        bank_free[keys[1]] = (sACT, v)

    def emit_PVg(h, j, qs):
        pbuf = RA + (j % 2) * 8192
        zk = ("PC", qs % 2)
        za = bank(PC, qs % 2)
        g = pe_group(
            za[:, 0:65], [zk],
            [(ar[:, pbuf + kc * 512 + qs * 128: pbuf + kc * 512 + (qs + 1) * 128],
              ar[:, KVT + kc * 65: KVT + (kc + 1) * 65])
             for kc in range(NKC)],
            waits=[(sACT, exp_done[(h, j)]), kvt_ready[0]])
        b.emit("vector", lambda e, a=za: e.tensor_copy(t1[:, 0:1], a[:, 64:65]),
               waits=[(sPE, g)], inc=sDVE)
        b.emit("vector", lambda e: e.reciprocal(rc, t1[:, 0:1]), inc=sDVE)
        tc = j * 4 + qs
        zev_done[(h, j, qs)] = consume(
            "vector", lambda e, a=za, tc=tc: e.tensor_scalar_mul(
                ar[:, UTM + tc * 64: UTM + (tc + 1) * 64], a[:, 0:64], rc),
            [zk], waits=(), inc=sDVE)

    def emit_UT(h, oct_):
        key = ("UT", h, oct_)
        if key in emitted:
            return
        emitted.add(key)
        P = PB
        keys = [("PB", 0), ("PB", 1)]
        w = [(sDVE, zev_done[(h, 2 * oct_ + 1, 3)]), (sPOOL, pool_init)]
        for k in keys:
            tf = bank_free.get(k)
            if tf:
                w.append(tf)
        r0 = (h % 2) * 64
        for i in range(8):
            tc = oct_ * 8 + i
            b.emit("tensor", lambda e, tc=tc, i=i, P=P, r0=r0: e.matmul(
                P[r0:r0 + 64, i * 128:(i + 1) * 128],
                ar[:, UTM + tc * 64: UTM + (tc + 1) * 64],
                ident, start=True, stop=True),
                waits=w if i == 0 else (),
                inc=sPE if i == 7 else None)
        g = sPE.n
        ub = UFM[(h // 2) % 2]
        dst = ar[r0:r0 + 64, ub + oct_ * 1024: ub + (oct_ + 1) * 1024]
        v = consume("scalar", lambda e, P=P, d=dst, r0=r0: e.activation(
            d, P[r0:r0 + 64, 0:1024], AF.Copy),
            keys, waits=[(sPE, g)], inc=sACT)
        ut_done[h % 2][oct_] = (sACT, v)

    def emit_WO2(h1, o, oct_):
        P, pk = ((PB, "PB"), (PC, "PC"))[wo_pair[0] % 2]
        wo_pair[0] += 1
        h0 = h1 - 1
        ps = (h1 // 2) % 2
        g = None
        for n in (0, 1):
            tb = oct_ * 2 + n
            g = pe_group(
                bank(P, n), [(pk, n)],
                [(ar[:, MSB[ps] + o * 128: MSB[ps] + (o + 1) * 128],
                  ar[:, UFM[ps] + tb * 512: UFM[ps] + (tb + 1) * 512])],
                waits=[msb_done[h0 % 4], msb_done[h1 % 4],
                       ut_done[0][oct_], ut_done[1][oct_]])
        wo_add_last[o] = consume(
            "vector", lambda e, P=P, o=o, oc=oct_: e.tensor_add(
                x[:, o * T + oc * 1024: o * T + (oc + 1) * 1024], P[:, 0:1024],
                x[:, o * T + oc * 1024: o * T + (oc + 1) * 1024]),
            [(pk, 0), (pk, 1)],
            waits=[(sPE, g)] + [(s, 16) for s in sX], inc=sDVE)

    def emit_ln(first, out_ready, pre=None, after_tb=None, pre_tb=None):
        xb_tb = [None] * NTB
        xsq_tb = [None] * NTB

        def emit_pre(tb):
            if pre is not None:
                xb_tb[tb] = list(pre[0][tb])
                xsq_tb[tb] = list(pre[1][tb])
                return
            x3 = (x.rearrange("p (c n) -> p c n", c=NCH)
                  [:, :, tb * 512:(tb + 1) * 512])
            xb3 = (ar[:, RA:RA + NCH * T].rearrange("p (c n) -> p c n", c=NCH)
                   [:, :, tb * 512:(tb + 1) * 512])
            xq3 = (ar[:, XH:XH + NCH * T].rearrange("p (c n) -> p c n", c=NCH)
                   [:, :, tb * 512:(tb + 1) * 512])
            cv = b.emit("scalar", lambda e, o=xb3, i=x3: e.activation(
                o, i, AF.Copy), waits=[(sX[tb], 16)], inc=sACT)
            sv = b.emit("scalar", lambda e, o=xq3, i=x3: e.activation(
                o, i, AF.Square), inc=sACT)
            xb_tb[tb] = [(sACT, cv)]
            xsq_tb[tb] = [(sACT, sv)]

        emit_pre(0)
        emit_pre(1)

        srow = [(PA[0:1, 0:512], PA[0:1, 512:1024], ("PA", 0), ("PA", 1)),
                (PD[0:1, 0:512], PD[0:1, 512:1024], ("PD", 0), ("PD", 1))]
        stats_done = [None] * NTB

        def emit_stats(tb):
            r0, r1, k0, k1 = srow[tb % 2]
            sum_done = pe_group(
                r0, [k0],
                [(ones_bf, ar[:, RA + c * T + tb * 512: RA + c * T + (tb + 1) * 512])
                 for c in range(NCH)],
                waits=list(xb_tb[tb]) + [(sPOOL, pool_init)])
            sumsq_done = pe_group(
                r1, [k1],
                [(ones_bf, ar[:, XH + c * T + tb * 512: XH + c * T + (tb + 1) * 512])
                 for c in range(NCH)],
                waits=list(xsq_tb[tb]))
            stats_done[tb] = (sum_done, sumsq_done)

        r0a, r1a, k0a, k1a = srow[0]
        r0b, r1b, k0b, k1b = srow[1]
        sum0 = pe_group(r0a, [k0a],
                        [(ones_bf, ar[:, RA + c * T: RA + c * T + 512])
                         for c in range(NCH)],
                        waits=list(xb_tb[0]) + [(sPOOL, pool_init)])
        sq0 = pe_group(r1a, [k1a],
                       [(ones_bf, ar[:, XH + c * T: XH + c * T + 512])
                        for c in range(NCH)],
                       waits=list(xsq_tb[0]))
        sum1 = pe_group(r0b, [k0b],
                        [(ones_bf, ar[:, RA + c * T + 512: RA + c * T + 1024])
                         for c in range(NCH)],
                        waits=list(xb_tb[1]))
        sq1 = pe_group(r1b, [k1b],
                       [(ones_bf, ar[:, XH + c * T + 512: XH + c * T + 1024])
                        for c in range(NCH)],
                       waits=list(xsq_tb[1]))
        stats_done[0] = (sum0, sq0)
        stats_done[1] = (sum1, sq1)

        stA = [None] * NTB
        stB = [None] * NTB

        def stage_A(tb):
            r0, r1, k0, k1 = srow[tb % 2]
            ro = (tb % 2) * 1536
            sum_done, sumsq_done = stats_done[tb]
            v_mean = b.emit(
                "vector", lambda e, r0=r0, ro=ro: e.tensor_scalar_mul(
                    rows[0:1, ro:ro + 512], r0, 1.0 / 1024.0),
                waits=[(sPE, sum_done)], inc=sDVE)
            b.emit("vector", lambda e, r0=r0, ro=ro: e.tensor_mul(
                rows[0:1, ro + 1024:ro + 1536], rows[0:1, ro:ro + 512], r0),
                inc=sDVE)
            v_var = b.emit("vector", lambda e, r1=r1, ro=ro: e.tensor_sub(
                rows[0:1, ro + 1024:ro + 1536], r1,
                rows[0:1, ro + 1024:ro + 1536]),
                waits=[(sPE, sumsq_done)], inc=sDVE)
            bank_free[k0] = (sDVE, v_var)
            bank_free[k1] = (sDVE, v_var)
            b.emit("vector", lambda e, ro=ro: e.tensor_scalar(
                rows[0:1, ro + 512:ro + 1024], rows[0:1, ro + 1024:ro + 1536],
                1.0 / 1024.0, EPS, ALU.mult, ALU.add), inc=sDVE)
            a_ln = b.emit("scalar", lambda e, ro=ro: e.activation(
                rows[0:1, ro + 1024:ro + 1536], rows[0:1, ro + 512:ro + 1024],
                AF.Ln), waits=[(sDVE, sDVE.n)], inc=sACT)
            a_rstd = b.emit("scalar", lambda e, ro=ro: e.activation(
                rows[0:1, ro + 512:ro + 1024], rows[0:1, ro + 1024:ro + 1536],
                AF.Exp, scale=-0.5), waits=[(sACT, a_ln)], inc=sACT)
            stA[tb] = (v_mean, a_rstd)

        def stage_B(tb):
            ro = (tb % 2) * 1536
            bo = (tb % 2) * 512
            mbcs = mbc[:, bo:bo + 512]
            rbcs = rbc[:, bo:bo + 512]
            v_mean, a_rstd = stA[tb]
            vm_b = b.emit("vector", lambda e, ro=ro, m=mbcs: e.tensor_copy(
                m[0:1, :], rows[0:1, ro:ro + 512]),
                waits=[(sDVE, v_mean)], inc=sDVE)
            vr_b = b.emit("vector", lambda e, ro=ro, r=rbcs: e.tensor_copy(
                r[0:1, :], rows[0:1, ro + 512:ro + 1024]),
                waits=[(sACT, a_rstd)], inc=sDVE)
            bc_m = pe_group(PB[:, 0:512], [("PB", 0)],
                            [(onesb_row, mbcs[0:1, :])],
                            waits=[(sDVE, vm_b), (sPOOL, pool_init)])
            bc_r = pe_group(PB[:, 512:1024], [("PB", 1)],
                            [(onesb_row, rbcs[0:1, :])],
                            waits=[(sDVE, vr_b)])
            if first:
                cm = consume("vector", lambda e, m=mbcs: e.tensor_copy(
                    m, PB[:, 0:512]), [("PB", 0)],
                    waits=[(sPE, bc_m)], inc=sDVE)
                cr = consume("vector", lambda e, r=rbcs: e.tensor_copy(
                    r, PB[:, 512:1024]), [("PB", 1)],
                    waits=[(sPE, bc_r)], inc=sDVE)
                crw = (sDVE, cr)
            else:
                cm = consume("scalar", lambda e, m=mbcs: e.activation(
                    m, PB[:, 0:512], AF.Copy), [("PB", 0)],
                    waits=[(sPE, bc_m)], inc=sACT)
                cr = consume("scalar", lambda e, r=rbcs: e.activation(
                    r, PB[:, 512:1024], AF.Copy), [("PB", 1)],
                    waits=[(sPE, bc_r)], inc=sACT)
                crw = (sACT, cr)
            stB[tb] = (mbcs, rbcs, crw)

        def stage_C(tb):
            mbcs, rbcs, crw = stB[tb]
            xh3 = (ar[:, XH:XH + NCH * T].rearrange("p (c n) -> p c n", c=NCH)
                   [:, :, tb * 512:(tb + 1) * 512])
            xb3 = (ar[:, RA:RA + NCH * T].rearrange("p (c n) -> p c n", c=NCH)
                   [:, :, tb * 512:(tb + 1) * 512])
            mb3 = mbcs.unsqueeze(1).broadcast_to([128, NCH, 512])
            rb3 = rbcs.unsqueeze(1).broadcast_to([128, NCH, 512])
            b.emit("vector", lambda e, o=xh3, a=xb3, m=mb3: e.tensor_sub(o, a, m),
                   waits=[crw], inc=sDVE)
            vv = b.emit("vector", lambda e, o=xh3, r=rb3: e.tensor_mul(o, o, r),
                        inc=sDVE)
            out_ready.append(vv)

        stage_A(0)
        stage_B(0)
        emit_pre(2)
        emit_stats(2)
        stage_A(1)
        stage_C(0)
        if pre_tb is not None:
            pre_tb(1)
        stage_B(1)
        emit_pre(3)
        emit_stats(3)
        stage_A(2)
        stage_C(1)
        if pre_tb is not None:
            pre_tb(2)
        stage_B(2)
        stage_A(3)
        stage_C(2)
        if pre_tb is not None:
            pre_tb(3)
        stage_B(3)
        stage_C(3)
        if after_tb is not None:
            after_tb(NTB - 1)
        return out_ready

    def ln1_pre(tb):
        if tb == 1:
            emit_kv(0)
            emit_prepA(0)
        elif tb == 2:
            emit_kv(1)
            emit_prepB(0, on_pa=True)
            emit_Qp(0, 0)
        elif tb == 3:
            emit_Spair(0, 0, 0)
            emit_Spair(0, 0, 2)
            emit_kv(2)
            emit_prepC(0)

    def ln1_after(tb):
        if tb == 3:
            emit_kv(3)
            emit_Spair(0, 0, 4)
            emit_kvt()

    emit_ln(True, xh_attn, after_tb=ln1_after, pre_tb=ln1_pre)
    maybe_tap("xh1", ar[:, XH:XH + NCH * T], [(sDVE, xh_attn[-1])])
    maybe_tap("kv", ar[0:64, KVA:KVA + T], [(sDVE, kv_ready_tb[-1])])
    maybe_tap("wab0", ar[0:64, WAB:WAB + 1024], [(sDVE, prepA_ev[0])])

    for h in range(H):
        nxt = h + 1
        emit_prepA(h)
        emit_prepB(h)
        emit_prepC(h)
        if h > 0:
            for tb in range(NTB):
                emit_Qp(h, tb)
        for kp in range(8):
            emit_Spair(h, 0, kp)
            if h == 0 and kp < 3:
                emit_Qp(0, kp + 1)
        if h == 0:
            maybe_tap("qp0", ar[0:64, QP:QP + 2048], [qpev[0][NTB - 1]])
            maybe_tap("p0", ar[:, RA:RA + 8192], [(sACT, exp_done[(0, 0)])])
        fill = {}
        if nxt < H:
            fill = {(1, 1): [lambda: emit_prepA(nxt)],
                    (2, 1): [lambda: emit_prepB(nxt)],
                    (2, 3): [lambda: emit_Qp(nxt, 0)],
                    (3, 0): [lambda: emit_Qp(nxt, 1)],
                    (3, 1): [lambda: emit_prepC(nxt)],
                    (3, 2): [lambda: emit_Qp(nxt, 2)],
                    (3, 3): [lambda: emit_Qp(nxt, 3)]}
        wu = []
        if h % 2 == 0 and h > 0:
            wu = [(h - 1, o, oc) for o in range(4) for oc in (0, 1)]
        elif h % 2 == 1 and h >= 3:
            wu = [(h - 2, o, oc) for o in range(4, NCH) for oc in (0, 1)]
        for idx, (h1, o, oc) in enumerate(wu):
            slot = (1 + idx // 4, idx % 4)
            fill.setdefault(slot, []).insert(0, (lambda h1=h1, o=o, oc=oc:
                                                 emit_WO2(h1, o, oc)))
        for j in (1, 2, 3):
            for i in range(4):
                emit_Spair(h, j, 2 * i)
                emit_Spair(h, j, 2 * i + 1)
                emit_PVg(h, j - 1, i)
                for f in fill.get((j, i), ()):
                    f()
        if h + 2 < H:
            load_wq(h + 2, extra_waits=[(sPE, q_mm_done[h])])
            load_wo(h + 2, extra_waits=[(sPE, wo_mm_done[h])])

        emit_PVg(h, 3, 0)
        if nxt < H:
            emit_Spair(nxt, 0, 0)
        emit_PVg(h, 3, 1)
        if nxt < H:
            emit_Spair(nxt, 0, 1)
        emit_PVg(h, 3, 2)
        if nxt < H:
            emit_Spair(nxt, 0, 2)
        emit_PVg(h, 3, 3)
        if nxt < H:
            emit_Spair(nxt, 0, 3)
        emit_UT(h, 0)
        if nxt < H:
            emit_Spair(nxt, 0, 4)
        emit_UT(h, 1)
        if nxt < H:
            emit_Spair(nxt, 0, 5)
            emit_Spair(nxt, 0, 6)
            emit_Spair(nxt, 0, 7)
            if h % 2 == 1:
                emit_Spair(nxt, 1, 0)
                emit_Spair(nxt, 1, 1)
                emit_Spair(nxt, 1, 2)
                emit_Spair(nxt, 1, 3)
        else:
            xb_l = [None] * NTB
            xsq_l = [None] * NTB
            for oct_ in (0, 1):
                for o in range(NCH):
                    emit_WO2(h, o, oct_)
                av = wo_add_last[NCH - 1]
                sh = (sXB, sXB2)[oct_]
                x3h = (x.rearrange("p (c n) -> p c n", c=NCH)
                       [:, :, oct_ * 1024:(oct_ + 1) * 1024])
                xb3h = (ar[:, RA:RA + NCH * T]
                        .rearrange("p (c n) -> p c n", c=NCH)
                        [:, :, oct_ * 1024:(oct_ + 1) * 1024])
                b.emit("gpsimd", lambda e, o=xb3h, i=x3h: e.dma_start(
                    out=o, in_=i),
                    waits=[(sDVE, av)], inc=sh, selfsync=False)
                for tb in (oct_ * 2, oct_ * 2 + 1):
                    x3 = (x.rearrange("p (c n) -> p c n", c=NCH)
                          [:, :, tb * 512:(tb + 1) * 512])
                    xq3 = (ar[:, XH:XH + NCH * T]
                           .rearrange("p (c n) -> p c n", c=NCH)
                           [:, :, tb * 512:(tb + 1) * 512])
                    sv = b.emit("scalar", lambda e, o=xq3, i=x3: e.activation(
                        o, i, AF.Square), waits=[(sDVE, av)], inc=sACT)
                    xb_l[tb] = [(sh, 16)]
                    xsq_l[tb] = [(sACT, sv)]
            pre_ln2[0] = (xb_l, xsq_l)
            wo_mm_done[H - 1] = sPE.n

        if h == 0:
            maybe_tap("u0", ar[:, UTM:UTM + 1024], [(sDVE, zev_done[(0, 3, 3)])])
            maybe_tap("uf0", ar[:, UFM[0]:UFM[0] + 2048], [ut_done[0][1]])

    pe_attn_done = sPE.n
    maybe_tap("xa", x[:, 0:NCH * T], [(sDVE, max(wo_add_last))])

    wg_load_done = [None] * NFC
    gate_done = [None] * NFC
    up_done = [None] * NFC
    hmul_done = [[None, None] for _ in range(NFC)]

    def load_wg(c, waits):
        b.emit("gpsimd", lambda e, c=c: e.dma_start(
            out=ar[:, WG[c % 2]:WG[c % 2] + 1024], in_=wg_d[c]),
            waits=waits, inc=sWG[c % 2], selfsync=False)

    def load_wu(c, waits):
        b.emit("gpsimd", lambda e, c=c: e.dma_start(
            out=ar[:, WU[c % 2]:WU[c % 2] + 1024], in_=wu_d[c]),
            waits=waits, inc=sWU[c % 2], selfsync=False)

    def load_wd(c, waits):
        ci = c % NHALF
        b.emit("gpsimd", lambda e, c=c, ci=ci: e.dma_start(
            out=ar[:, WDB + ci * 1024:WDB + (ci + 1) * 1024],
            in_=wd_d[c]),
            waits=waits, inc=sWD, selfsync=False)

    load_wg(0, [(sPE, q_mm_done[6])])
    load_wg(1, [])
    load_wu(0, [(sPE, q_mm_done[7])])
    load_wu(1, [])
    for c in range(NHALF):
        load_wd(c, [(sPE, wo_mm_done[7])] if c == 0 else [])

    G_ap = [PA, PC]
    U_ap = [PB, PD]
    G_keys = [[("PA", 0), ("PA", 1)], [("PC", 0), ("PC", 1)]]
    U_keys = [[("PB", 0), ("PB", 1)], [("PD", 0), ("PD", 1)]]

    scr_free = [None]
    down_b1_done = [0]

    def emit_ffn_A(c, t):
        key = ("A", c, t)
        if key in emitted:
            return
        emitted.add(key)
        half = c // NHALF
        ci = c % NHALF
        gg = None
        for n in range(2):
            tb = t * 2 + n
            gg = pe_group(
                bank(G_ap[t], n), [G_keys[t][n]],
                [(ar[:, WG[c % 2] + k * 128: WG[c % 2] + (k + 1) * 128],
                  ar[:, XH + k * T + tb * 512: XH + k * T + (tb + 1) * 512])
                 for k in range(NCH)],
                waits=[(sWG[c % 2], 16 * (c // 2 + 1)), (sDVE, xh_ffn[tb])])
        gate_done[c] = max(gate_done[c] or 0, gg)
        ug = None
        for n in range(2):
            tb = t * 2 + n
            ug = pe_group(
                bank(U_ap[t], n), [U_keys[t][n]],
                [(ar[:, WU[c % 2] + k * 128: WU[c % 2] + (k + 1) * 128],
                  ar[:, XH + k * T + tb * 512: XH + k * T + (tb + 1) * 512])
                 for k in range(NCH)],
                waits=[(sWU[c % 2], 16 * (c // 2 + 1)), (sDVE, xh_ffn[tb])])
        up_done[c] = max(up_done[c] or 0, ug)
        scr_w = [(sPE, wo_mm_done[6])] if scr_free[0] is None else [scr_free[0]]
        a_si = consume(
            "scalar", lambda e, t=t: e.activation(
                ar[:, SCR_S:SCR_S + 1024], G_ap[t][:, 0:1024], AF.Silu,
                scale=INV127),
            G_keys[t], waits=[(sPE, gg)] + scr_w, inc=sACT)
        a_up = consume(
            "scalar", lambda e, t=t: e.activation(
                ar[:, SCR_U:SCR_U + 1024], U_ap[t][:, 0:1024], AF.Copy,
                scale=INV127 * INV127),
            U_keys[t], waits=[(sPE, ug)], inc=sACT)
        hcol = HHB + ci * T + t * 1024
        hw = [(sACT, a_up)]
        if half == 1:
            hw.append((sPE, down_b1_done[0]))
        hmul_done[c][t] = b.emit(
            "vector", lambda e, hcol=hcol: e.tensor_mul(
                ar[:, hcol:hcol + 1024],
                ar[:, SCR_S:SCR_S + 1024], ar[:, SCR_U:SCR_U + 1024]),
            waits=hw, inc=sDVE)
        scr_free[0] = (sDVE, hmul_done[c][t])
        if t == 1 and c + 2 < NFC:
            load_wg(c + 2, [(sPE, gate_done[c])])
            load_wu(c + 2, [(sPE, up_done[c])])

    def ln2_pre(tb):
        if tb == 2:
            emit_ffn_A(0, 0)
        elif tb == 3:
            emit_ffn_A(1, 0)

    emit_ln(False, xh_ffn, pre=pre_ln2[0], pre_tb=ln2_pre)
    maybe_tap("xh2", ar[:, XH:XH + NCH * T], [(sDVE, xh_ffn[-1])])

    down_add_final = None
    for half in range(2):
        c0 = half * NHALF
        order = [(0, 0), (1, 0), (0, 1), (1, 1)]
        order += [(ci, t) for ci in range(2, NHALF) for t in range(2)]
        for (ci, t) in order:
            emit_ffn_A(c0 + ci, t)
        if half == 0:
            maybe_tap("h0", ar[:, HHB:HHB + NHALF * T],
                      [(sDVE, hmul_done[NHALF - 1][1])])

        d_rot = 0
        for o in range(NCH):
            for tp in range(2):
                P, pk = ((PA, "PA"), (PB, "PB"))[d_rot % 2]
                d_rot += 1
                g = None
                for n in (0, 1):
                    tb = tp * 2 + n
                    g = pe_group(
                        bank(P, n), [(pk, n)],
                        [(ar[:, WDB + ci * 1024 + o * 128: WDB + ci * 1024 + (o + 1) * 128],
                          ar[:, HHB + ci * T + tb * 512: HHB + ci * T + (tb + 1) * 512])
                         for ci in range(NHALF)],
                        waits=[(sWD, 16 * NHALF * (half + 1)),
                               (sDVE, hmul_done[c0 + NHALF - 1][tp])])
                down_add_final = consume(
                    "vector", lambda e, P=P, o=o, tp=tp: e.tensor_add(
                        x[:, o * T + tp * 1024: o * T + (tp + 1) * 1024], P[:, 0:1024],
                        x[:, o * T + tp * 1024: o * T + (tp + 1) * 1024]),
                    [(pk, 0), (pk, 1)], waits=[(sPE, g)], inc=sDVE)
            if half == 1:
                b.emit("sync", lambda e, o=o: e.dma_start(
                    out=out_d[o * 128:(o + 1) * 128, :],
                    in_=x[:, o * T:(o + 1) * T]),
                    waits=[(sDVE, down_add_final)], inc=sST)
        down_b1_done[0] = sPE.n
        if half == 0:
            for c in range(NHALF, NFC):
                load_wd(c, [(sPE, down_b1_done[0])] if c == NHALF else [])

    fin = sST.n
    b.emit("sync", lambda e: e.wait_ge(sST.h, fin))

    with nc.Block() as block:
        @block.tensor
        def _(eng):
            for f in b.q["tensor"]:
                f(eng)

        @block.scalar
        def _(eng):
            for f in b.q["scalar"]:
                f(eng)

        @block.vector
        def _(eng):
            for f in b.q["vector"]:
                f(eng)

        @block.gpsimd
        def _(eng):
            for f in b.q["gpsimd"]:
                f(eng)

        @block.sync
        def _(eng):
            for f in b.q["sync"]:
                f(eng)

    return nc


def _prep_inputs(L0_attn_x, L0_attn_Wq, L0_attn_Wkv_down, L0_attn_Wk_up,
                 L0_attn_Wv_up, L0_attn_Wo, L0_ffn_w_gate, L0_ffn_w_up,
                 L0_ffn_w_down):
    xs = np.asarray(L0_attn_x)
    Wq = np.asarray(L0_attn_Wq)
    Wkv = np.asarray(L0_attn_Wkv_down)
    Wku = np.asarray(L0_attn_Wk_up)
    Wvu = np.asarray(L0_attn_Wv_up)
    Wo = np.asarray(L0_attn_Wo)
    Wg = np.asarray(L0_ffn_w_gate)
    Wu = np.asarray(L0_ffn_w_up)
    Wd = np.asarray(L0_ffn_w_down)

    wq = np.ascontiguousarray(
        Wq.reshape(H, 2, 128, D).transpose(0, 2, 1, 3).reshape(H, 128, 2048))
    wkv = np.ascontiguousarray(
        Wkv.T.reshape(NCH, 128, LR).transpose(1, 0, 2).reshape(128, 512))
    wku = np.ascontiguousarray(
        Wku.reshape(H, 2, 128, LR).transpose(2, 0, 1, 3).reshape(128, 1024))
    wvu = np.ascontiguousarray(
        Wvu.reshape(H, 2, 128, LR).transpose(2, 0, 1, 3).reshape(128, 1024))
    wo = np.ascontiguousarray(
        Wo.T.reshape(H, 2, 128, D).transpose(0, 2, 1, 3).reshape(H, 128, 2048))
    wg = np.ascontiguousarray(
        Wg.T.reshape(NCH, 128, NFC, 128).transpose(2, 1, 0, 3).reshape(NFC, 128, 1024))
    wu = np.ascontiguousarray(
        Wu.T.reshape(NCH, 128, NFC, 128).transpose(2, 1, 0, 3).reshape(NFC, 128, 1024))
    wd = np.ascontiguousarray(Wd.T.reshape(NFC, 128, D))
    maps = []
    for i in range(xs.shape[0]):
        xq = np.ascontiguousarray(
            xs[i].T.reshape(NCH, 128, NTB, 512).transpose(2, 1, 0, 3))
        maps.append(dict(
            x=xq, wq=wq, wkv=wkv, wku=wku,
            wvu=wvu, wo=wo, wg=wg, wu=wu, wd=wd))
    return maps


def kernel(L0_attn_x, L0_attn_Wq, L0_attn_Wkv_down, L0_attn_Wk_up,
           L0_attn_Wv_up, L0_attn_Wo, L0_ffn_w_gate, L0_ffn_w_up,
           L0_ffn_w_down):
    maps = _prep_inputs(L0_attn_x, L0_attn_Wq, L0_attn_Wkv_down, L0_attn_Wk_up,
                        L0_attn_Wv_up, L0_attn_Wo, L0_ffn_w_gate, L0_ffn_w_up,
                        L0_ffn_w_down)
    nc = build_program()
    res = run_bass_kernel_spmd(nc, maps, core_ids=list(range(8)))
    out = np.stack([np.asarray(r["out"]).T for r in res.results])
    return np.ascontiguousarray(out.astype(np.float32))


# version 46 (final)
# speedup vs baseline: 1.0215x; 1.0099x over previous
import numpy as np

import concourse.bass as bass
import concourse.mybir as mybir
from concourse.bass_utils import run_bass_kernel_spmd

F32 = mybir.dt.float32
BF16 = mybir.dt.bfloat16
I32 = mybir.dt.int32
AF = mybir.ActivationFunctionType
ALU = mybir.AluOpType

T = 2048
D = 1024
H = 8
DH = 256
LR = 64
FH = 3072
NCH = 8
NTB = 4
NKC = 16
NFC = 24
NHALF = 12
INV127 = 1.0 / 127.0
SC_ATTN = 1.0 / 16.0
EPS = 1e-5

XH = 0
RA = 16384
QH = 32768
KH = 36864
VH = 40960
ZT = 45072
ZF = 49168
WQ0, WQ1 = 53264, 55312
WO0, WO1 = 57360, 59408
WKU = 43008
WVU = 44048
WKV = 50192
KVA = 50704
IDC = 52752
ONE = 53248
ARENA = 61456
QP = QH
KVT = KH
WAB = KH + 1040
WAT = (KH + 2064, KH + 2576)
MSB = (VH, VH + 1024)
UTM = 45072
UFM = (46096, 48144)
HHB = RA
WDB = VH
WG = (WQ0, WQ0 + 1024)
WU = (WQ1, WQ1 + 1024)
SCR_S = WO0
SCR_U = WO0 + 1024


class _Sem:
    __slots__ = ("h", "step", "n")


class _Builder:
    def __init__(self, nc):
        self.nc = nc
        self.q = {"tensor": [], "scalar": [], "vector": [], "gpsimd": [], "sync": []}
        self.waited = {k: {} for k in self.q}
        self.selfsem = {}

    def sem(self, name, step=1):
        s = _Sem()
        s.h = self.nc.alloc_semaphore(name)
        s.step = step
        s.n = 0
        return s

    def emit(self, eng, fn, waits=(), inc=None, selfsync=True):
        ss = self.selfsem.get(eng)
        if ss is not None and selfsync:
            waits = list(waits) + ([(ss, ss.n)] if ss.n > 0 else [])
            if inc is None:
                inc = ss
            else:
                assert inc is ss, f"{eng} compute ops must inc the engine sem"
        wl = []
        for (s, v) in waits:
            if s is None or v is None or v <= 0:
                continue
            assert v <= s.n, f"wait beyond emitted count on {eng}: {v} > {s.n}"
            w = self.waited[eng]
            k = id(s)
            if v > w.get(k, 0):
                w[k] = v
                wl.append((s.h, v))
        after = None
        if inc is not None:
            inc.n += inc.step
            after = (inc.h, inc.step)

        def run(e, wl=wl, fn=fn, after=after):
            for (hh, v) in wl:
                e.wait_ge(hh, v)
            inst = fn(e)
            if after is not None:
                inst.then_inc(after[0], after[1])

        self.q[eng].append(run)
        return inc.n if inc is not None else None


def build_program(tap=None):
    nc = bass.Bass("TRN2", target_bir_lowering=False)

    x_d = nc.dram_tensor("x", [NTB, 128, NCH, 512], F32, kind="ExternalInput").ap()
    wq_d = nc.dram_tensor("wq", [H, 128, 2048], F32, kind="ExternalInput").ap()
    wkv_d = nc.dram_tensor("wkv", [128, 512], F32, kind="ExternalInput").ap()
    wku_d = nc.dram_tensor("wku", [128, 1024], F32, kind="ExternalInput").ap()
    wvu_d = nc.dram_tensor("wvu", [128, 1024], F32, kind="ExternalInput").ap()
    wo_d = nc.dram_tensor("wo", [H, 128, 2048], F32, kind="ExternalInput").ap()
    wg_d = nc.dram_tensor("wg", [NFC, 128, 1024], I32, kind="ExternalInput").ap()
    wu_d = nc.dram_tensor("wu", [NFC, 128, 1024], I32, kind="ExternalInput").ap()
    wd_d = nc.dram_tensor("wd", [NFC, 128, D], I32, kind="ExternalInput").ap()
    out_d = nc.dram_tensor("out", [D, T], F32, kind="ExternalOutput").ap()

    x_t = nc.alloc_sbuf_tensor("x_fm", [128, NCH * T], F32)
    ar_t = nc.alloc_sbuf_tensor("arena", [128, ARENA], BF16)
    t1_t = nc.alloc_sbuf_tensor("t1", [128, 16], F32)
    rows_t = nc.alloc_sbuf_tensor("rows", [1, 3072], F32)
    rc_t = nc.alloc_sbuf_tensor("rc", [128, 1], F32)
    mbc_t = nc.alloc_sbuf_tensor("mbc", [128, 1024], BF16)
    rbc_t = nc.alloc_sbuf_tensor("rbc", [128, 1024], BF16)
    onesb_row_t = nc.alloc_sbuf_tensor("onesb_row", [1, 128], BF16)
    x = x_t.ap()
    ar = ar_t.ap()
    t1 = t1_t.ap()
    rows = rows_t.ap()
    rc = rc_t.ap()
    mbc = mbc_t.ap()
    rbc = rbc_t.ap()
    onesb_row = onesb_row_t.ap()

    PA = nc.alloc_psum_tensor("PA", [128, 1024], F32).ap()
    PB = nc.alloc_psum_tensor("PB", [128, 1024], F32).ap()
    PC = nc.alloc_psum_tensor("PC", [128, 1024], F32).ap()
    PD = nc.alloc_psum_tensor("PD", [128, 1024], F32).ap()

    def bank(P, i):
        return P[:, i * 512:(i + 1) * 512]

    b = _Builder(nc)
    sX = [b.sem(f"sX{i}", 16) for i in range(4)]
    sWS = b.sem("sWS", 16)
    sXB = b.sem("sXB", 16)
    sXB2 = b.sem("sXB2", 16)
    sWQ = [b.sem("sWQ0", 16), b.sem("sWQ1", 16)]
    sWO = [b.sem("sWO0", 16), b.sem("sWO1", 16)]
    sWG = [b.sem("sWG0", 16), b.sem("sWG1", 16)]
    sWU = [b.sem("sWU0", 16), b.sem("sWU1", 16)]
    sWD = b.sem("sWD", 16)
    sST = b.sem("sST", 16)
    sPE = b.sem("sPE", 1)
    sACT = b.sem("sACT", 1)
    sDVE = b.sem("sDVE", 1)
    sPOOL = b.sem("sPOOL", 1)
    b.selfsem = {"vector": sDVE, "scalar": sACT, "gpsimd": sPOOL}

    bank_free = {}

    def pe_group(out_ap, keys, pieces, waits=()):
        w = list(waits)
        for k in keys:
            bf = bank_free.get(k)
            if bf:
                w.append(bf)
        n = len(pieces)
        for i, (l, r) in enumerate(pieces):
            b.emit(
                "tensor",
                (lambda e, o=out_ap, l=l, r=r, st=(i == 0), sp=(i == n - 1):
                 e.matmul(o, l, r, start=st, stop=sp)),
                waits=w if i == 0 else (),
                inc=sPE if i == n - 1 else None,
            )
        return sPE.n

    def consume(eng, fn, keys, waits, inc):
        v = b.emit(eng, fn, waits=waits, inc=inc)
        for k in keys:
            bank_free[k] = (inc, v)
        return v

    def maybe_tap(name, src_ap, waits):
        if tap == name:
            td = nc.dram_tensor("tap", list(src_ap.shape), src_ap.dtype,
                                kind="ExternalOutput").ap()
            b.emit("sync", lambda e, s=src_ap, t=td: e.dma_start(out=t, in_=s),
                   waits=waits, inc=sST)

    wz = b.emit("gpsimd", lambda e: e.memset(ar[:, QH:QH + 512], 0.0), inc=sPOOL)
    ident = ar[:, IDC:IDC + 128]
    ones_bf = ar[:, ONE:ONE + 1]
    b.emit("gpsimd", lambda e: e.memset(ident, 0.0), inc=sPOOL)
    b.emit("gpsimd", lambda e: e.affine_select(
        out=ident, in_=ident,
        compare_op=ALU.not_equal, fill=1.0, base=0,
        pattern=[[-1, 128]], channel_multiplier=1), inc=sPOOL)
    b.emit("gpsimd", lambda e: e.memset(ones_bf, 1.0), inc=sPOOL)
    b.emit("gpsimd", lambda e: e.memset(onesb_row, 1.0), inc=sPOOL)
    kvt1 = ar[:, KVT:KVT + NKC * 65].rearrange("p (t c) -> p t c", c=65)[:, :, 64:65]
    b.emit("gpsimd", lambda e: e.memset(kvt1, 1.0), inc=sPOOL)
    b.emit("gpsimd", lambda e: e.memset(ar[64:128, KVA:KVA + 2048], 0.0), inc=sPOOL)
    pool_init = sPOOL.n
    for tb in range(NTB):
        b.emit("sync", lambda e, tb=tb: e.dma_start(
            out=x.rearrange("p (c n) -> p c n", c=NCH)
                [:, :, tb * 512:(tb + 1) * 512],
            in_=x_d[tb]),
            inc=sX[tb])
    for i in range(63):
        b.emit("tensor", lambda e: e.matmul(
            PD[:, 0:512], ar[:, QH:QH + 128], ar[:, QH:QH + 512],
            start=True, stop=True),
            waits=[(sPOOL, wz)] if i == 0 else (), inc=None)
    b.emit("tensor", lambda e: e.matmul(
        PD[:, 0:512], ar[:, QH:QH + 128], ar[:, QH:QH + 512],
        start=True, stop=True), inc=sPE)
    warm_done = sPE.n
    bank_free[("PD", 0)] = (sPE, warm_done)
    b.emit("gpsimd", lambda e: e.dma_start(
        out=ar[:, WKV:WKV + 512], in_=wkv_d), inc=sWS, selfsync=False)
    b.emit("gpsimd", lambda e: e.dma_start(
        out=ar[:, WKU:WKU + 1024], in_=wku_d), inc=sWS, selfsync=False)
    b.emit("gpsimd", lambda e: e.dma_start(
        out=ar[:, WVU:WVU + 1024], in_=wvu_d), inc=sWS, selfsync=False)
    ws_all = sWS.n

    wq_base = [WQ0, WQ1]
    wo_base = [WO0, WO1]

    def load_wq(h, extra_waits=()):
        b.emit("gpsimd", lambda e, h=h: e.dma_start(
            out=ar[:, wq_base[h % 2]:wq_base[h % 2] + 2048], in_=wq_d[h]),
            waits=extra_waits, inc=sWQ[h % 2], selfsync=False)

    def load_wo(h, extra_waits=()):
        b.emit("gpsimd", lambda e, h=h: e.dma_start(
            out=ar[:, wo_base[h % 2]:wo_base[h % 2] + 2048], in_=wo_d[h]),
            waits=extra_waits, inc=sWO[h % 2], selfsync=False)

    load_wq(0)
    load_wo(0)
    load_wq(1, extra_waits=[(sX[2], 16)])
    load_wo(1)
    b.emit("gpsimd", lambda e: e.memset(ar[64:128, QP:QP + 4096], 0.0),
           waits=[(sPE, warm_done)], inc=sPOOL, selfsync=False)
    pool_qp = sPOOL.n

    xh_attn = []
    xh_ffn = []
    kv_ready_tb = [None] * NTB
    kvt_ready = [None]
    qpev = [[None] * NTB for _ in range(H)]
    prepA_ev = [None] * H
    wat_done = [None, None]
    msb_done = [None] * 4
    ut_done = [[None, None] for _ in range(2)]
    q_mm_done = [0] * H
    wo_mm_done = [0] * H
    wo_add_last = [0] * NCH
    pb_rot = [0]
    s_rot = [0]
    wo_pair = [0]
    exp_done = {}
    zev_done = {}
    emitted = set()
    pre_ln2 = [None]

    proj_set = [[("PB", 0), ("PB", 1)]]

    def proj_bank():
        banks = proj_set[0]
        k = banks[pb_rot[0] % len(banks)]
        pb_rot[0] += 1
        return k, bank({"PB": PB, "PC": PC}[k[0]], k[1])

    def emit_prepA(h):
        key = ("prepA", h)
        if key in emitted:
            return
        emitted.add(key)
        wqb = wq_base[h % 2]
        g = None
        for half in (0, 1):
            g = pe_group(
                PD[0:64, half * 512:(half + 1) * 512], [("PD", half)],
                [(ar[:, WKU + (h * 2 + c) * 64: WKU + (h * 2 + c + 1) * 64],
                  ar[:, wqb + c * 1024 + half * 512:
                     wqb + c * 1024 + (half + 1) * 512])
                 for c in (0, 1)],
                waits=[(sWS, ws_all), (sWQ[h % 2], 16 * (h // 2 + 1))])
        q_mm_done[h] = g
        prepA_ev[h] = consume("vector", lambda e: e.tensor_copy(
            ar[0:64, WAB:WAB + 1024], PD[0:64, 0:1024]),
            [("PD", 0), ("PD", 1)], waits=[(sPE, g)], inc=sDVE)

    def emit_prepB(h, on_pa=False):
        key = ("prepB", h)
        if key in emitted:
            return
        emitted.add(key)
        P, bk = (PA, ("PA", 0)) if on_pa else (PD, ("PD", 0))
        w = [(sDVE, prepA_ev[h]), (sPOOL, pool_init)]
        tf = bank_free.get(bk)
        if tf:
            w.append(tf)
        for i in range(8):
            b.emit("tensor", lambda e, i=i, P=P: e.matmul(
                P[:, i * 64:(i + 1) * 64],
                ar[0:64, WAB + i * 128: WAB + (i + 1) * 128],
                ident[0:64, 0:64], start=True, stop=True),
                waits=w if i == 0 else (),
                inc=sPE if i == 7 else None)
        g = sPE.n
        v = consume("vector", lambda e, P=P, h=h: e.tensor_copy(
            ar[:, WAT[h % 2]:WAT[h % 2] + 512], P[:, 0:512]),
            [bk], waits=[(sPE, g)], inc=sDVE)
        wat_done[h % 2] = (sDVE, v)

    def emit_prepC(h):
        key = ("prepC", h)
        if key in emitted:
            return
        emitted.add(key)
        wob = wo_base[h % 2]
        r0 = (h % 2) * 64
        g = None
        for half in (0, 1):
            g = pe_group(
                PD[r0:r0 + 64, half * 512:(half + 1) * 512], [("PD", half)],
                [(ar[:, WVU + (h * 2 + c) * 64: WVU + (h * 2 + c + 1) * 64],
                  ar[:, wob + c * 1024 + half * 512:
                     wob + c * 1024 + (half + 1) * 512])
                 for c in (0, 1)],
                waits=[(sWS, ws_all), (sWO[h % 2], 16 * (h // 2 + 1))])
        wo_mm_done[h] = g
        ms = MSB[(h // 2) % 2]
        v = consume("vector", lambda e, r0=r0, ms=ms: e.tensor_copy(
            ar[r0:r0 + 64, ms:ms + 1024], PD[r0:r0 + 64, 0:1024]),
            [("PD", 0), ("PD", 1)], waits=[(sPE, g)], inc=sDVE)
        msb_done[h % 4] = (sDVE, v)

    def emit_Qp(h, tb, eng="vector"):
        key = ("QP", h, tb)
        if key in emitted:
            return
        emitted.add(key)
        qk, qa = proj_bank()
        g = pe_group(
            qa[0:64, :], [qk],
            [(ar[:, WAT[h % 2] + c * 64: WAT[h % 2] + (c + 1) * 64],
              ar[:, XH + c * T + tb * 512: XH + c * T + (tb + 1) * 512])
             for c in range(NCH)],
            waits=[wat_done[h % 2], (sDVE, xh_attn[tb])])
        qpb = QP + (h % 2) * 2048
        dst = ar[0:64, qpb + tb * 512: qpb + (tb + 1) * 512]
        if eng == "vector":
            v = consume("vector", lambda e, a=qa, d=dst: e.tensor_copy(
                d, a[0:64, :]), [qk], waits=[(sPE, g)], inc=sDVE)
            qpev[h][tb] = (sDVE, v)
        else:
            v = consume("scalar", lambda e, a=qa, d=dst: e.activation(
                d, a[0:64, :], AF.Copy), [qk], waits=[(sPE, g)], inc=sACT)
            qpev[h][tb] = (sACT, v)

    def emit_kv(tb):
        g = pe_group(
            bank(PC, tb % 2)[0:64, :], [("PC", tb % 2)],
            [(ar[:, WKV + c * 64:WKV + (c + 1) * 64],
              ar[:, XH + c * T + tb * 512: XH + c * T + (tb + 1) * 512])
             for c in range(NCH)],
            waits=[(sWS, ws_all), (sDVE, xh_attn[tb])])
        kv_ready_tb[tb] = consume(
            "vector", lambda e, tb=tb: e.tensor_copy(
                ar[0:64, KVA + tb * 512:KVA + (tb + 1) * 512],
                bank(PC, tb % 2)[0:64, :]),
            [("PC", tb % 2)], waits=[(sPE, g)], inc=sDVE)

    def emit_kvt():
        key = ("kvt",)
        if key in emitted:
            return
        emitted.add(key)
        for oct_ in (0, 1):
            P = (PA, PD)[oct_]
            bk = ("PA", 0) if oct_ == 0 else ("PD", 0)
            w = [(sDVE, kv_ready_tb[NTB - 1]), (sPOOL, pool_init)]
            tf = bank_free.get(bk)
            if tf:
                w.append(tf)
            for i in range(8):
                kc = oct_ * 8 + i
                b.emit("tensor", lambda e, kc=kc, i=i, P=P: e.matmul(
                    P[:, i * 64:(i + 1) * 64],
                    ar[0:64, KVA + kc * 128: KVA + (kc + 1) * 128],
                    ident[0:64, 0:64], start=True, stop=True),
                    waits=w if i == 0 else (),
                    inc=sPE if i == 7 else None)
            g = sPE.n
            dst = (ar[:, KVT + oct_ * 8 * 65: KVT + (oct_ * 8 + 8) * 65]
                   .rearrange("p (t c) -> p t c", c=65)[:, :, 0:64])
            sc = P[:, 0:512].rearrange("p (t c) -> p t c", c=64)
            consume("vector", lambda e, d=dst, s=sc: e.tensor_copy(d, s),
                    [bk], waits=[(sPE, g)], inc=sDVE)
        kvt_ready[0] = (sDVE, sDVE.n)

    def emit_Spair(h, j, kp):
        key = ("SP", h, j, kp)
        if key in emitted:
            return
        emitted.add(key)
        bi = s_rot[0] % 2
        s_rot[0] += 1
        P = (PA, PD)[bi]
        keys = [("PA", 0), ("PA", 1)] if bi == 0 else [("PD", 0), ("PD", 1)]
        qpb = QP + (h % 2) * 2048
        dw = [(sPOOL, pool_init)]
        tf = bank_free.get(keys[0])
        if tf:
            dw.append(tf)
        b.emit("tensor", lambda e, P=P: e.matmul(
            bank(P, 0), ident, ar[:, XH:XH + 512], start=True, stop=True),
            waits=dw, inc=None)
        g = None
        for hf in (0, 1):
            kc = 2 * kp + hf
            g = pe_group(
                bank(P, hf), [keys[hf]],
                [(ar[:, KVA + kc * 128: KVA + (kc + 1) * 128],
                  ar[:, qpb + j * 512: qpb + (j + 1) * 512])],
                waits=[qpev[h][j], (sDVE, kv_ready_tb[kp // 2]),
                       (sPOOL, pool_qp)])
        pbuf = RA + (j % 2) * 8192
        ew = [(sPE, g)]
        if h == 0 and j <= 1:
            ew.append((sDVE, xh_attn[1 if kp % 2 == 0 else NTB - 1]))
        v = b.emit(
            "scalar", lambda e, P=P, p=pbuf, kp=kp: e.activation(
                ar[:, p + kp * 1024: p + (kp + 1) * 1024], P[:, 0:1024],
                AF.Exp, scale=SC_ATTN),
            waits=ew, inc=sACT)
        exp_done[(h, j)] = v
        bank_free[keys[0]] = (sACT, v)
        bank_free[keys[1]] = (sACT, v)

    def emit_PVg(h, j, qs):
        pbuf = RA + (j % 2) * 8192
        zk = ("PC", qs % 2)
        za = bank(PC, qs % 2)
        g = pe_group(
            za[:, 0:65], [zk],
            [(ar[:, pbuf + kc * 512 + qs * 128: pbuf + kc * 512 + (qs + 1) * 128],
              ar[:, KVT + kc * 65: KVT + (kc + 1) * 65])
             for kc in range(NKC)],
            waits=[(sACT, exp_done[(h, j)]), kvt_ready[0]])
        b.emit("vector", lambda e, a=za: e.tensor_copy(t1[:, 0:1], a[:, 64:65]),
               waits=[(sPE, g)], inc=sDVE)
        b.emit("vector", lambda e: e.reciprocal(rc, t1[:, 0:1]), inc=sDVE)
        tc = j * 4 + qs
        zev_done[(h, j, qs)] = consume(
            "vector", lambda e, a=za, tc=tc: e.tensor_scalar_mul(
                ar[:, UTM + tc * 64: UTM + (tc + 1) * 64], a[:, 0:64], rc),
            [zk], waits=(), inc=sDVE)

    def emit_UT(h, oct_):
        key = ("UT", h, oct_)
        if key in emitted:
            return
        emitted.add(key)
        P = PB
        keys = [("PB", 0), ("PB", 1)]
        w = [(sDVE, zev_done[(h, 2 * oct_ + 1, 3)]), (sPOOL, pool_init)]
        for k in keys:
            tf = bank_free.get(k)
            if tf:
                w.append(tf)
        r0 = (h % 2) * 64
        for i in range(8):
            tc = oct_ * 8 + i
            b.emit("tensor", lambda e, tc=tc, i=i, P=P, r0=r0: e.matmul(
                P[r0:r0 + 64, i * 128:(i + 1) * 128],
                ar[:, UTM + tc * 64: UTM + (tc + 1) * 64],
                ident, start=True, stop=True),
                waits=w if i == 0 else (),
                inc=sPE if i == 7 else None)
        g = sPE.n
        ub = UFM[(h // 2) % 2]
        dst = ar[r0:r0 + 64, ub + oct_ * 1024: ub + (oct_ + 1) * 1024]
        v = consume("vector", lambda e, P=P, d=dst, r0=r0: e.tensor_copy(
            d, P[r0:r0 + 64, 0:1024]),
            keys, waits=[(sPE, g)], inc=sDVE)
        ut_done[h % 2][oct_] = (sDVE, v)

    def emit_WO2(h1, o, oct_):
        P, pk = ((PB, "PB"), (PC, "PC"))[wo_pair[0] % 2]
        wo_pair[0] += 1
        h0 = h1 - 1
        ps = (h1 // 2) % 2
        g = None
        for n in (0, 1):
            tb = oct_ * 2 + n
            g = pe_group(
                bank(P, n), [(pk, n)],
                [(ar[:, MSB[ps] + o * 128: MSB[ps] + (o + 1) * 128],
                  ar[:, UFM[ps] + tb * 512: UFM[ps] + (tb + 1) * 512])],
                waits=[msb_done[h0 % 4], msb_done[h1 % 4],
                       ut_done[0][oct_], ut_done[1][oct_]])
        wo_add_last[o] = consume(
            "vector", lambda e, P=P, o=o, oc=oct_: e.tensor_add(
                x[:, o * T + oc * 1024: o * T + (oc + 1) * 1024], P[:, 0:1024],
                x[:, o * T + oc * 1024: o * T + (oc + 1) * 1024]),
            [(pk, 0), (pk, 1)],
            waits=[(sPE, g)] + [(s, 16) for s in sX], inc=sDVE)

    def emit_ln(first, out_ready, pre=None, after_tb=None, pre_tb=None):
        xb_tb = [None] * NTB
        xsq_tb = [None] * NTB

        def emit_pre(tb):
            if pre is not None:
                xb_tb[tb] = list(pre[0][tb])
                xsq_tb[tb] = list(pre[1][tb])
                return
            x3 = (x.rearrange("p (c n) -> p c n", c=NCH)
                  [:, :, tb * 512:(tb + 1) * 512])
            xb3 = (ar[:, RA:RA + NCH * T].rearrange("p (c n) -> p c n", c=NCH)
                   [:, :, tb * 512:(tb + 1) * 512])
            xq3 = (ar[:, XH:XH + NCH * T].rearrange("p (c n) -> p c n", c=NCH)
                   [:, :, tb * 512:(tb + 1) * 512])
            cv = b.emit("scalar", lambda e, o=xb3, i=x3: e.activation(
                o, i, AF.Copy), waits=[(sX[tb], 16)], inc=sACT)
            sv = b.emit("scalar", lambda e, o=xq3, i=x3: e.activation(
                o, i, AF.Square), inc=sACT)
            xb_tb[tb] = [(sACT, cv)]
            xsq_tb[tb] = [(sACT, sv)]

        emit_pre(0)
        emit_pre(1)

        srow = [(PA[0:1, 0:512], PA[0:1, 512:1024], ("PA", 0), ("PA", 1)),
                (PD[0:1, 0:512], PD[0:1, 512:1024], ("PD", 0), ("PD", 1))]
        stats_done = [None] * NTB

        def emit_stats(tb):
            r0, r1, k0, k1 = srow[tb % 2]
            sum_done = pe_group(
                r0, [k0],
                [(ones_bf, ar[:, RA + c * T + tb * 512: RA + c * T + (tb + 1) * 512])
                 for c in range(NCH)],
                waits=list(xb_tb[tb]) + [(sPOOL, pool_init)])
            sumsq_done = pe_group(
                r1, [k1],
                [(ones_bf, ar[:, XH + c * T + tb * 512: XH + c * T + (tb + 1) * 512])
                 for c in range(NCH)],
                waits=list(xsq_tb[tb]))
            stats_done[tb] = (sum_done, sumsq_done)

        r0a, r1a, k0a, k1a = srow[0]
        r0b, r1b, k0b, k1b = srow[1]
        sum0 = pe_group(r0a, [k0a],
                        [(ones_bf, ar[:, RA + c * T: RA + c * T + 512])
                         for c in range(NCH)],
                        waits=list(xb_tb[0]) + [(sPOOL, pool_init)])
        sq0 = pe_group(r1a, [k1a],
                       [(ones_bf, ar[:, XH + c * T: XH + c * T + 512])
                        for c in range(NCH)],
                       waits=list(xsq_tb[0]))
        sum1 = pe_group(r0b, [k0b],
                        [(ones_bf, ar[:, RA + c * T + 512: RA + c * T + 1024])
                         for c in range(NCH)],
                        waits=list(xb_tb[1]))
        sq1 = pe_group(r1b, [k1b],
                       [(ones_bf, ar[:, XH + c * T + 512: XH + c * T + 1024])
                        for c in range(NCH)],
                       waits=list(xsq_tb[1]))
        stats_done[0] = (sum0, sq0)
        stats_done[1] = (sum1, sq1)

        stA = [None] * NTB
        stB = [None] * NTB

        def stage_A(tb):
            r0, r1, k0, k1 = srow[tb % 2]
            ro = (tb % 2) * 1536
            sum_done, sumsq_done = stats_done[tb]
            v_mean = b.emit(
                "vector", lambda e, r0=r0, ro=ro: e.tensor_scalar_mul(
                    rows[0:1, ro:ro + 512], r0, 1.0 / 1024.0),
                waits=[(sPE, sum_done)], inc=sDVE)
            b.emit("vector", lambda e, r0=r0, ro=ro: e.tensor_mul(
                rows[0:1, ro + 1024:ro + 1536], rows[0:1, ro:ro + 512], r0),
                inc=sDVE)
            v_var = b.emit("vector", lambda e, r1=r1, ro=ro: e.tensor_sub(
                rows[0:1, ro + 1024:ro + 1536], r1,
                rows[0:1, ro + 1024:ro + 1536]),
                waits=[(sPE, sumsq_done)], inc=sDVE)
            bank_free[k0] = (sDVE, v_var)
            bank_free[k1] = (sDVE, v_var)
            b.emit("vector", lambda e, ro=ro: e.tensor_scalar(
                rows[0:1, ro + 512:ro + 1024], rows[0:1, ro + 1024:ro + 1536],
                1.0 / 1024.0, EPS, ALU.mult, ALU.add), inc=sDVE)
            a_ln = b.emit("scalar", lambda e, ro=ro: e.activation(
                rows[0:1, ro + 1024:ro + 1536], rows[0:1, ro + 512:ro + 1024],
                AF.Ln), waits=[(sDVE, sDVE.n)], inc=sACT)
            a_rstd = b.emit("scalar", lambda e, ro=ro: e.activation(
                rows[0:1, ro + 512:ro + 1024], rows[0:1, ro + 1024:ro + 1536],
                AF.Exp, scale=-0.5), waits=[(sACT, a_ln)], inc=sACT)
            stA[tb] = (v_mean, a_rstd)

        def stage_B(tb):
            ro = (tb % 2) * 1536
            bo = (tb % 2) * 512
            mbcs = mbc[:, bo:bo + 512]
            rbcs = rbc[:, bo:bo + 512]
            v_mean, a_rstd = stA[tb]
            vm_b = b.emit("vector", lambda e, ro=ro, m=mbcs: e.tensor_copy(
                m[0:1, :], rows[0:1, ro:ro + 512]),
                waits=[(sDVE, v_mean)], inc=sDVE)
            vr_b = b.emit("vector", lambda e, ro=ro, r=rbcs: e.tensor_copy(
                r[0:1, :], rows[0:1, ro + 512:ro + 1024]),
                waits=[(sACT, a_rstd)], inc=sDVE)
            bc_m = pe_group(PB[:, 0:512], [("PB", 0)],
                            [(onesb_row, mbcs[0:1, :])],
                            waits=[(sDVE, vm_b), (sPOOL, pool_init)])
            bc_r = pe_group(PB[:, 512:1024], [("PB", 1)],
                            [(onesb_row, rbcs[0:1, :])],
                            waits=[(sDVE, vr_b)])
            if first:
                cm = consume("vector", lambda e, m=mbcs: e.tensor_copy(
                    m, PB[:, 0:512]), [("PB", 0)],
                    waits=[(sPE, bc_m)], inc=sDVE)
                cr = consume("vector", lambda e, r=rbcs: e.tensor_copy(
                    r, PB[:, 512:1024]), [("PB", 1)],
                    waits=[(sPE, bc_r)], inc=sDVE)
                crw = (sDVE, cr)
            else:
                cm = consume("scalar", lambda e, m=mbcs: e.activation(
                    m, PB[:, 0:512], AF.Copy), [("PB", 0)],
                    waits=[(sPE, bc_m)], inc=sACT)
                cr = consume("scalar", lambda e, r=rbcs: e.activation(
                    r, PB[:, 512:1024], AF.Copy), [("PB", 1)],
                    waits=[(sPE, bc_r)], inc=sACT)
                crw = (sACT, cr)
            stB[tb] = (mbcs, rbcs, crw)

        def stage_C(tb):
            mbcs, rbcs, crw = stB[tb]
            xh3 = (ar[:, XH:XH + NCH * T].rearrange("p (c n) -> p c n", c=NCH)
                   [:, :, tb * 512:(tb + 1) * 512])
            xb3 = (ar[:, RA:RA + NCH * T].rearrange("p (c n) -> p c n", c=NCH)
                   [:, :, tb * 512:(tb + 1) * 512])
            mb3 = mbcs.unsqueeze(1).broadcast_to([128, NCH, 512])
            rb3 = rbcs.unsqueeze(1).broadcast_to([128, NCH, 512])
            b.emit("vector", lambda e, o=xh3, a=xb3, m=mb3: e.tensor_sub(o, a, m),
                   waits=[crw], inc=sDVE)
            vv = b.emit("vector", lambda e, o=xh3, r=rb3: e.tensor_mul(o, o, r),
                        inc=sDVE)
            out_ready.append(vv)

        stage_A(0)
        stage_B(0)
        emit_pre(2)
        emit_stats(2)
        stage_A(1)
        stage_C(0)
        if pre_tb is not None:
            pre_tb(1)
        stage_B(1)
        emit_pre(3)
        emit_stats(3)
        stage_A(2)
        stage_C(1)
        if pre_tb is not None:
            pre_tb(2)
        stage_B(2)
        stage_A(3)
        stage_C(2)
        if pre_tb is not None:
            pre_tb(3)
        stage_B(3)
        stage_C(3)
        if after_tb is not None:
            after_tb(NTB - 1)
        return out_ready

    def ln1_pre(tb):
        if tb == 1:
            emit_kv(0)
            emit_prepA(0)
        elif tb == 2:
            emit_kv(1)
            emit_prepB(0, on_pa=True)
            emit_Qp(0, 0)
        elif tb == 3:
            emit_Spair(0, 0, 0)
            emit_Spair(0, 0, 2)
            emit_kv(2)
            emit_prepC(0)

    def ln1_after(tb):
        if tb == 3:
            emit_kv(3)
            emit_Spair(0, 0, 4)
            emit_kvt()

    emit_ln(True, xh_attn, after_tb=ln1_after, pre_tb=ln1_pre)
    maybe_tap("xh1", ar[:, XH:XH + NCH * T], [(sDVE, xh_attn[-1])])
    maybe_tap("kv", ar[0:64, KVA:KVA + T], [(sDVE, kv_ready_tb[-1])])
    maybe_tap("wab0", ar[0:64, WAB:WAB + 1024], [(sDVE, prepA_ev[0])])

    for h in range(H):
        nxt = h + 1
        emit_prepA(h)
        emit_prepB(h)
        emit_prepC(h)
        if h > 0:
            for tb in range(NTB):
                emit_Qp(h, tb)
        for kp in range(8):
            emit_Spair(h, 0, kp)
            if h == 0 and kp < 3:
                emit_Qp(0, kp + 1)
        if h == 0:
            maybe_tap("qp0", ar[0:64, QP:QP + 2048], [qpev[0][NTB - 1]])
            maybe_tap("p0", ar[:, RA:RA + 8192], [(sACT, exp_done[(0, 0)])])
        fill = {}
        if nxt < H:
            fill = {(1, 1): [lambda: emit_prepA(nxt)],
                    (2, 1): [lambda: emit_prepB(nxt)],
                    (2, 3): [lambda: emit_Qp(nxt, 0)],
                    (3, 0): [lambda: emit_Qp(nxt, 1)],
                    (3, 1): [lambda: emit_prepC(nxt)],
                    (3, 2): [lambda: emit_Qp(nxt, 2)],
                    (3, 3): [lambda: emit_Qp(nxt, 3)]}
        wu = []
        if h % 2 == 0 and h > 0:
            wu = [(h - 1, o, oc) for o in range(4) for oc in (0, 1)]
        elif h % 2 == 1 and h >= 3:
            wu = [(h - 2, o, oc) for o in range(4, NCH) for oc in (0, 1)]
        for idx, (h1, o, oc) in enumerate(wu):
            slot = (1 + idx // 4, idx % 4)
            fill.setdefault(slot, []).insert(0, (lambda h1=h1, o=o, oc=oc:
                                                 emit_WO2(h1, o, oc)))
        for j in (1, 2, 3):
            for i in range(4):
                emit_Spair(h, j, 2 * i)
                emit_Spair(h, j, 2 * i + 1)
                emit_PVg(h, j - 1, i)
                for f in fill.get((j, i), ()):
                    f()
        if h + 2 < H:
            load_wq(h + 2, extra_waits=[(sPE, q_mm_done[h])])
            load_wo(h + 2, extra_waits=[(sPE, wo_mm_done[h])])

        emit_PVg(h, 3, 0)
        if nxt < H:
            emit_Spair(nxt, 0, 0)
        emit_PVg(h, 3, 1)
        if nxt < H:
            emit_Spair(nxt, 0, 1)
        emit_PVg(h, 3, 2)
        if nxt < H:
            emit_Spair(nxt, 0, 2)
        emit_PVg(h, 3, 3)
        if nxt < H:
            emit_Spair(nxt, 0, 3)
        emit_UT(h, 0)
        if nxt < H:
            emit_Spair(nxt, 0, 4)
        emit_UT(h, 1)
        if nxt < H:
            emit_Spair(nxt, 0, 5)
            emit_Spair(nxt, 0, 6)
            emit_Spair(nxt, 0, 7)
            if h % 2 == 1:
                emit_Spair(nxt, 1, 0)
                emit_Spair(nxt, 1, 1)
                emit_Spair(nxt, 1, 2)
                emit_Spair(nxt, 1, 3)
        else:
            xb_l = [None] * NTB
            xsq_l = [None] * NTB
            for oct_ in (0, 1):
                for o in range(NCH):
                    emit_WO2(h, o, oct_)
                av = wo_add_last[NCH - 1]
                sh = (sXB, sXB2)[oct_]
                x3h = (x.rearrange("p (c n) -> p c n", c=NCH)
                       [:, :, oct_ * 1024:(oct_ + 1) * 1024])
                xb3h = (ar[:, RA:RA + NCH * T]
                        .rearrange("p (c n) -> p c n", c=NCH)
                        [:, :, oct_ * 1024:(oct_ + 1) * 1024])
                b.emit("gpsimd", lambda e, o=xb3h, i=x3h: e.dma_start(
                    out=o, in_=i),
                    waits=[(sDVE, av)], inc=sh, selfsync=False)
                for tb in (oct_ * 2, oct_ * 2 + 1):
                    x3 = (x.rearrange("p (c n) -> p c n", c=NCH)
                          [:, :, tb * 512:(tb + 1) * 512])
                    xq3 = (ar[:, XH:XH + NCH * T]
                           .rearrange("p (c n) -> p c n", c=NCH)
                           [:, :, tb * 512:(tb + 1) * 512])
                    sv = b.emit("scalar", lambda e, o=xq3, i=x3: e.activation(
                        o, i, AF.Square), waits=[(sDVE, av)], inc=sACT)
                    xb_l[tb] = [(sh, 16)]
                    xsq_l[tb] = [(sACT, sv)]
            pre_ln2[0] = (xb_l, xsq_l)
            wo_mm_done[H - 1] = sPE.n

        if h == 0:
            maybe_tap("u0", ar[:, UTM:UTM + 1024], [(sDVE, zev_done[(0, 3, 3)])])
            maybe_tap("uf0", ar[:, UFM[0]:UFM[0] + 2048], [ut_done[0][1]])

    pe_attn_done = sPE.n
    maybe_tap("xa", x[:, 0:NCH * T], [(sDVE, max(wo_add_last))])

    wg_load_done = [None] * NFC
    gate_done = [None] * NFC
    up_done = [None] * NFC
    hmul_done = [[None, None] for _ in range(NFC)]

    def load_wg(c, waits):
        b.emit("gpsimd", lambda e, c=c: e.dma_start(
            out=ar[:, WG[c % 2]:WG[c % 2] + 1024], in_=wg_d[c]),
            waits=waits, inc=sWG[c % 2], selfsync=False)

    def load_wu(c, waits):
        b.emit("gpsimd", lambda e, c=c: e.dma_start(
            out=ar[:, WU[c % 2]:WU[c % 2] + 1024], in_=wu_d[c]),
            waits=waits, inc=sWU[c % 2], selfsync=False)

    def load_wd(c, waits):
        ci = c % NHALF
        b.emit("gpsimd", lambda e, c=c, ci=ci: e.dma_start(
            out=ar[:, WDB + ci * 1024:WDB + (ci + 1) * 1024],
            in_=wd_d[c]),
            waits=waits, inc=sWD, selfsync=False)

    load_wg(0, [(sPE, q_mm_done[6])])
    load_wg(1, [])
    load_wu(0, [(sPE, q_mm_done[7])])
    load_wu(1, [])
    for c in range(NHALF):
        load_wd(c, [(sPE, wo_mm_done[7])] if c == 0 else [])

    G_ap = [PA, PC]
    U_ap = [PB, PD]
    G_keys = [[("PA", 0), ("PA", 1)], [("PC", 0), ("PC", 1)]]
    U_keys = [[("PB", 0), ("PB", 1)], [("PD", 0), ("PD", 1)]]

    scr_free = [None]
    down_b1_done = [0]

    def emit_ffn_A(c, t):
        key = ("A", c, t)
        if key in emitted:
            return
        emitted.add(key)
        half = c // NHALF
        ci = c % NHALF
        gg = None
        for n in range(2):
            tb = t * 2 + n
            gg = pe_group(
                bank(G_ap[t], n), [G_keys[t][n]],
                [(ar[:, WG[c % 2] + k * 128: WG[c % 2] + (k + 1) * 128],
                  ar[:, XH + k * T + tb * 512: XH + k * T + (tb + 1) * 512])
                 for k in range(NCH)],
                waits=[(sWG[c % 2], 16 * (c // 2 + 1)), (sDVE, xh_ffn[tb])])
        gate_done[c] = max(gate_done[c] or 0, gg)
        ug = None
        for n in range(2):
            tb = t * 2 + n
            ug = pe_group(
                bank(U_ap[t], n), [U_keys[t][n]],
                [(ar[:, WU[c % 2] + k * 128: WU[c % 2] + (k + 1) * 128],
                  ar[:, XH + k * T + tb * 512: XH + k * T + (tb + 1) * 512])
                 for k in range(NCH)],
                waits=[(sWU[c % 2], 16 * (c // 2 + 1)), (sDVE, xh_ffn[tb])])
        up_done[c] = max(up_done[c] or 0, ug)
        scr_w = [(sPE, wo_mm_done[6])] if scr_free[0] is None else [scr_free[0]]
        a_si = consume(
            "scalar", lambda e, t=t: e.activation(
                ar[:, SCR_S:SCR_S + 1024], G_ap[t][:, 0:1024], AF.Silu,
                scale=INV127),
            G_keys[t], waits=[(sPE, gg)] + scr_w, inc=sACT)
        a_up = consume(
            "scalar", lambda e, t=t: e.activation(
                ar[:, SCR_U:SCR_U + 1024], U_ap[t][:, 0:1024], AF.Copy,
                scale=INV127 * INV127),
            U_keys[t], waits=[(sPE, ug)], inc=sACT)
        hcol = HHB + ci * T + t * 1024
        hw = [(sACT, a_up)]
        if half == 1:
            hw.append((sPE, down_b1_done[0]))
        hmul_done[c][t] = b.emit(
            "vector", lambda e, hcol=hcol: e.tensor_mul(
                ar[:, hcol:hcol + 1024],
                ar[:, SCR_S:SCR_S + 1024], ar[:, SCR_U:SCR_U + 1024]),
            waits=hw, inc=sDVE)
        scr_free[0] = (sDVE, hmul_done[c][t])
        if t == 1 and c + 2 < NFC:
            load_wg(c + 2, [(sPE, gate_done[c])])
            load_wu(c + 2, [(sPE, up_done[c])])

    def ln2_pre(tb):
        if tb == 2:
            emit_ffn_A(0, 0)
        elif tb == 3:
            emit_ffn_A(1, 0)

    emit_ln(False, xh_ffn, pre=pre_ln2[0], pre_tb=ln2_pre)
    maybe_tap("xh2", ar[:, XH:XH + NCH * T], [(sDVE, xh_ffn[-1])])

    down_add_final = None
    for half in range(2):
        c0 = half * NHALF
        order = [(0, 0), (1, 0), (0, 1), (1, 1)]
        order += [(ci, t) for ci in range(2, NHALF) for t in range(2)]
        for (ci, t) in order:
            emit_ffn_A(c0 + ci, t)
        if half == 0:
            maybe_tap("h0", ar[:, HHB:HHB + NHALF * T],
                      [(sDVE, hmul_done[NHALF - 1][1])])

        d_rot = 0
        for o in range(NCH):
            for tp in range(2):
                P, pk = ((PA, "PA"), (PB, "PB"))[d_rot % 2]
                d_rot += 1
                g = None
                for n in (0, 1):
                    tb = tp * 2 + n
                    g = pe_group(
                        bank(P, n), [(pk, n)],
                        [(ar[:, WDB + ci * 1024 + o * 128: WDB + ci * 1024 + (o + 1) * 128],
                          ar[:, HHB + ci * T + tb * 512: HHB + ci * T + (tb + 1) * 512])
                         for ci in range(NHALF)],
                        waits=[(sWD, 16 * NHALF * (half + 1)),
                               (sDVE, hmul_done[c0 + NHALF - 1][tp])])
                down_add_final = consume(
                    "vector", lambda e, P=P, o=o, tp=tp: e.tensor_add(
                        x[:, o * T + tp * 1024: o * T + (tp + 1) * 1024], P[:, 0:1024],
                        x[:, o * T + tp * 1024: o * T + (tp + 1) * 1024]),
                    [(pk, 0), (pk, 1)], waits=[(sPE, g)], inc=sDVE)
            if half == 1:
                b.emit("sync", lambda e, o=o: e.dma_start(
                    out=out_d[o * 128:(o + 1) * 128, :],
                    in_=x[:, o * T:(o + 1) * T]),
                    waits=[(sDVE, down_add_final)], inc=sST)
        down_b1_done[0] = sPE.n
        if half == 0:
            for c in range(NHALF, NFC):
                load_wd(c, [(sPE, down_b1_done[0])] if c == NHALF else [])

    fin = sST.n
    b.emit("sync", lambda e: e.wait_ge(sST.h, fin))

    with nc.Block() as block:
        @block.tensor
        def _(eng):
            for f in b.q["tensor"]:
                f(eng)

        @block.scalar
        def _(eng):
            for f in b.q["scalar"]:
                f(eng)

        @block.vector
        def _(eng):
            for f in b.q["vector"]:
                f(eng)

        @block.gpsimd
        def _(eng):
            for f in b.q["gpsimd"]:
                f(eng)

        @block.sync
        def _(eng):
            for f in b.q["sync"]:
                f(eng)

    return nc


def _prep_inputs(L0_attn_x, L0_attn_Wq, L0_attn_Wkv_down, L0_attn_Wk_up,
                 L0_attn_Wv_up, L0_attn_Wo, L0_ffn_w_gate, L0_ffn_w_up,
                 L0_ffn_w_down):
    xs = np.asarray(L0_attn_x)
    Wq = np.asarray(L0_attn_Wq)
    Wkv = np.asarray(L0_attn_Wkv_down)
    Wku = np.asarray(L0_attn_Wk_up)
    Wvu = np.asarray(L0_attn_Wv_up)
    Wo = np.asarray(L0_attn_Wo)
    Wg = np.asarray(L0_ffn_w_gate)
    Wu = np.asarray(L0_ffn_w_up)
    Wd = np.asarray(L0_ffn_w_down)

    wq = np.ascontiguousarray(
        Wq.reshape(H, 2, 128, D).transpose(0, 2, 1, 3).reshape(H, 128, 2048))
    wkv = np.ascontiguousarray(
        Wkv.T.reshape(NCH, 128, LR).transpose(1, 0, 2).reshape(128, 512))
    wku = np.ascontiguousarray(
        Wku.reshape(H, 2, 128, LR).transpose(2, 0, 1, 3).reshape(128, 1024))
    wvu = np.ascontiguousarray(
        Wvu.reshape(H, 2, 128, LR).transpose(2, 0, 1, 3).reshape(128, 1024))
    wo = np.ascontiguousarray(
        Wo.T.reshape(H, 2, 128, D).transpose(0, 2, 1, 3).reshape(H, 128, 2048))
    wg = np.ascontiguousarray(
        Wg.T.reshape(NCH, 128, NFC, 128).transpose(2, 1, 0, 3).reshape(NFC, 128, 1024))
    wu = np.ascontiguousarray(
        Wu.T.reshape(NCH, 128, NFC, 128).transpose(2, 1, 0, 3).reshape(NFC, 128, 1024))
    wd = np.ascontiguousarray(Wd.T.reshape(NFC, 128, D))
    maps = []
    for i in range(xs.shape[0]):
        xq = np.ascontiguousarray(
            xs[i].T.reshape(NCH, 128, NTB, 512).transpose(2, 1, 0, 3))
        maps.append(dict(
            x=xq, wq=wq, wkv=wkv, wku=wku,
            wvu=wvu, wo=wo, wg=wg, wu=wu, wd=wd))
    return maps


def kernel(L0_attn_x, L0_attn_Wq, L0_attn_Wkv_down, L0_attn_Wk_up,
           L0_attn_Wv_up, L0_attn_Wo, L0_ffn_w_gate, L0_ffn_w_up,
           L0_ffn_w_down):
    maps = _prep_inputs(L0_attn_x, L0_attn_Wq, L0_attn_Wkv_down, L0_attn_Wk_up,
                        L0_attn_Wv_up, L0_attn_Wo, L0_ffn_w_gate, L0_ffn_w_up,
                        L0_ffn_w_down)
    nc = build_program()
    res = run_bass_kernel_spmd(nc, maps, core_ids=list(range(8)))
    out = np.stack([np.asarray(r["out"]).T for r in res.results])
    return np.ascontiguousarray(out.astype(np.float32))
